# version 13 (final)
# baseline (speedup 1.0000x reference)
import numpy as np
import ml_dtypes
from contextlib import ExitStack

import concourse.bass as bass
import concourse.mybir as mybir
from concourse.bass_utils import run_bass_kernel_spmd

F32 = mybir.dt.float32
BF16 = mybir.dt.bfloat16
AF = mybir.ActivationFunctionType
ALU = mybir.AluOpType
AX = mybir.AxisListType
ET = mybir.EngineType

NCORE = 8
B = 2
TSEQ = 8192
TOKC = 1024
NTILE = 16
LN_EPS = 1e-5
RMS_EPS = 1.1920929e-07
BIG = 1.0e6

bf16 = ml_dtypes.bfloat16
ENGS = ("sync", "tensor", "scalar", "vector", "gpsimd")
ENG_ET = {"sync": ET.SP, "tensor": ET.PE, "scalar": ET.Activation,
          "vector": ET.DVE, "gpsimd": ET.Pool}


class Planner:

    def __init__(self):
        self.segments = [{e: [] for e in ENGS}]
        self.count = {}
        self.chans = set()

    @property
    def ops(self):
        return self.segments[-1]

    def op(self, eng, fn, waits=(), token=True, n_dma=0, chan=None):
        if n_dma:
            assert chan is not None, "DMA op needs a channel"
            key = chan
            self.chans.add(chan)
            k = 16 * n_dma
        else:
            key = eng
            k = 1 if token else 0
        tok = None
        if k:
            self.count[key] = self.count.get(key, 0) + k
            tok = (key, self.count[key])
        self.ops[eng].append((fn, tuple(w for w in waits if w is not None), k,
                              key))
        return tok

    def collective(self, fn, waits=(), chan=None):
        self.chans.add(chan)
        self.count[chan] = self.count.get(chan, 0) + 1
        tok = (chan, self.count[chan])
        self.segments.append({e: [] for e in ENGS})
        self.ops["gpsimd"].append(
            (fn, tuple(w for w in waits if w is not None), 1, chan))
        self.segments.append({e: [] for e in ENGS})
        return tok

    def emit(self, block, sems):
        for seg in self.segments:
            for eng in ENGS:
                ops = seg[eng]
                if not ops:
                    continue

                def body(h, _ops=ops, _eng=eng):
                    for fn, waits, k, key in _ops:
                        for (we, wv) in waits:
                            h.wait_ge(sems[we], wv)
                        instr = fn(h)
                        if k:
                            assert instr is not None, \
                                f"op on {_eng} returned None"
                            instr.then_inc(sems[key], k)
                getattr(block, eng)(body)


def _host_inputs(inp, core):
    x = inp["L0_attn_x"]
    d = {}
    d["x_own"] = np.ascontiguousarray(
        np.stack([x[b, TOKC * core:TOKC * (core + 1)] for b in range(B)]))
    d["wdqT"] = np.ascontiguousarray(
        np.concatenate([inp["L0_attn_W_DQ"].T, inp["L0_attn_W_DQ"].T], axis=0))
    d["wuqT"] = np.ascontiguousarray(
        inp["L0_attn_W_UQ"].T.reshape(4, 128, 64).transpose(1, 0, 2))
    d["wdkvT"] = np.ascontiguousarray(
        np.concatenate([inp["L0_attn_W_DKV"].T, inp["L0_attn_W_DKV"].T], axis=0))
    d["wukvT"] = np.ascontiguousarray(
        inp["L0_attn_W_UKV"].T.reshape(8, 128, 128).transpose(1, 0, 2))
    wqr = inp["L0_attn_W_QR"]
    wkr = inp["L0_attn_W_KR"]
    z = np.zeros((128, 128), np.float32)
    z[0:64, 0:64] = wqr.T
    z[64:128, 64:128] = wqr.T
    d["wqr_bd"] = z.copy()
    z[:] = 0
    z[0:64, 0:64] = wkr.T
    z[64:128, 64:128] = wkr.T
    d["wkr_bd"] = z.copy()
    d["wqr_raw"] = np.ascontiguousarray(wqr)
    d["wkr_raw"] = np.ascontiguousarray(wkr)
    d["woT"] = np.ascontiguousarray(inp["L0_attn_W_O"].T)
    z = np.zeros((128, 128), np.float32)
    z[0:64, 0:64] = inp["L0_attn_W_O"].T
    z[64:128, 64:128] = inp["L0_attn_W_O"].T
    d["wo_bd"] = z.copy()
    qnw = inp["L0_attn_q_norm_w"]
    knw = inp["L0_attn_k_norm_w"]
    d["qnw_row"] = qnw[None, :].copy()
    d["knw_row"] = knw[None, :].copy()
    z = np.zeros((2, 128), np.float32)
    z[0, 0:64] = qnw
    z[1, 64:128] = qnw
    d["qbcast2"] = z.copy()
    z[:] = 0
    z[0, 0:64] = knw
    z[1, 64:128] = knw
    d["kbcast2"] = z.copy()
    d["gwT"] = np.ascontiguousarray(
        inp["L0_ffn_gate_weight"].T.reshape(8, 128, 8).transpose(1, 0, 2))
    d["gbias"] = np.ascontiguousarray(inp["L0_ffn_gate_bias"][:, None])
    perm = np.empty(2048, np.int64)
    for c in range(8):
        ii = np.arange(128) + 128 * c
        perm[256 * c:256 * c + 128] = 2 * ii
        perm[256 * c + 128:256 * c + 256] = 2 * ii + 1
    d["w1pT"] = np.ascontiguousarray(
        inp["L0_ffn_mlp1_weight"].transpose(0, 2, 1)[:, :, perm])
    b1p = inp["L0_ffn_mlp1_bias"][:, perm].reshape(8, 8, 2, 128)
    d["b1g_cols"] = np.ascontiguousarray(
        b1p[:, :, 0, :].transpose(2, 0, 1).reshape(128, 64))
    d["b1l_cols"] = np.ascontiguousarray(
        b1p[:, :, 1, :].transpose(2, 0, 1).reshape(128, 64))
    d["w2T"] = np.ascontiguousarray(inp["L0_ffn_mlp2_weight"].transpose(0, 2, 1))
    d["b2"] = np.ascontiguousarray(inp["L0_ffn_mlp2_bias"])
    d["nw_cols"] = np.ascontiguousarray(inp["L0_ffn_norm_weight"].reshape(8, 128).T)
    d["ident"] = np.eye(128, dtype=bf16)
    d["ident32"] = np.eye(128, dtype=np.float32)
    d["masks"] = (np.arange(128)[:, None]
                  <= (np.arange(896)[None, :] - 384)).astype(bf16)
    d["iota8"] = np.tile(np.arange(8, dtype=np.float32), (128, 1))
    d["iotaB"] = d["iota8"] + BIG
    d["ones1x128"] = np.ones((1, 128), np.float32)
    d["ones64"] = np.ones((64, 1), np.float32)
    z = np.zeros((128, 2), np.float32)
    z[0:64, 0] = 1.0
    z[64:128, 1] = 1.0
    d["ones2"] = z
    sel = np.zeros((8, 8, 128), bf16)
    for e in range(8):
        sel[e, e, :] = bf16(1.0)
    d["selE"] = sel
    return d


def build_program(debug_dumps=(), stop_after=None):
    nc = bass.Bass("TRN2", target_bir_lowering=False,
                   dynamic_dma_scratch_size=4096)

    def din(name, shape, dtype):
        return nc.dram_tensor(name, shape, dtype, kind="ExternalInput").ap()

    x_own = din("x_own", [B, TOKC, 1024], F32)
    wdqT = din("wdqT", [128, 512], F32)
    wuqT = din("wuqT", [128, 4, 64], F32)
    wdkvT = din("wdkvT", [128, 1024], F32)
    wukvT = din("wukvT", [128, 8, 128], F32)
    wqr_bd = din("wqr_bd", [128, 128], F32)
    wkr_bd = din("wkr_bd", [128, 128], F32)
    wqr_raw = din("wqr_raw", [64, 64], F32)
    wkr_raw = din("wkr_raw", [64, 64], F32)
    woT = din("woT", [64, 64], F32)
    wo_bd = din("wo_bd", [128, 128], F32)
    qnw_row = din("qnw_row", [1, 64], F32)
    knw_row = din("knw_row", [1, 64], F32)
    qbcast2 = din("qbcast2", [2, 128], F32)
    kbcast2 = din("kbcast2", [2, 128], F32)
    gwT_in = din("gwT", [128, 8, 8], BF16)
    gbias = din("gbias", [8, 1], BF16)
    w1pT = din("w1pT", [8, 1024, 2048], BF16)
    b1g_cols = din("b1g_cols", [128, 64], BF16)
    b1l_cols = din("b1l_cols", [128, 64], BF16)
    w2T = din("w2T", [8, 1024, 1024], BF16)
    b2_in = din("b2", [8, 1024], BF16)
    nw_cols = din("nw_cols", [128, 8], F32)
    ident_in = din("ident", [128, 128], BF16)
    ident32_in = din("ident32", [128, 128], F32)
    masks_in = din("masks", [128, 896], BF16)
    iota8_in = din("iota8", [128, 8], F32)
    iotaB_in = din("iotaB", [128, 8], F32)
    ones1x128 = din("ones1x128", [1, 128], F32)
    ones64 = din("ones64", [64, 1], F32)
    ones2_in = din("ones2", [128, 2], F32)
    selE_in = din("selE", [8, 8, 128], BF16)

    out_own = nc.dram_tensor("out_own", [B, TOKC, 1024], F32,
                             kind="ExternalOutput").ap()
    ag1_in = nc.dram_tensor("ag1_in", [8 * 128, 2048], BF16).ap()
    ag1_out = nc.dram_tensor("ag1_out", [NCORE * 8 * 128, 2048], BF16,
                             addr_space="Shared").ap()
    ag2_in = nc.dram_tensor("ag2_in", [B * TSEQ, 128], BF16).ap()
    ahat_d = nc.dram_tensor("ahat_d", [4, 8, 8, 128, 512], BF16).ap()
    ag2_out = nc.dram_tensor("ag2_out", [NCORE * B * TSEQ, 128], BF16,
                             addr_space="Shared").ap()

    dbg_drams = {}
    for (name, shape, dtype) in debug_dumps:
        dbg_drams[name] = nc.dram_tensor("dbg_" + name, shape, dtype,
                                         kind="ExternalOutput").ap()

    P = Planner()
    st = ExitStack()

    def sb(name, shape, dtype):
        return st.enter_context(nc.sbuf_tensor(name, shape, dtype))

    xc = sb("xc", [128, NTILE, 1024], BF16)
    nfT = sb("nfT", [128, 8, 2048], BF16)
    kvq = sb("kvq", [128, 49408], BF16)
    xr = sb("xr", [128, 2, 1024], F32)
    w_dq = sb("w_dq", [128, 512], BF16)
    w_uq = sb("w_uq", [128, 4, 64], BF16)
    w_dkv = sb("w_dkv", [128, 1024], BF16)
    w_ukv = sb("w_ukv", [128, 8, 128], BF16)
    w_qrbd = sb("w_qrbd", [128, 128], BF16)
    w_krbd = sb("w_krbd", [128, 128], BF16)
    w_wobd = sb("w_wobd", [128, 128], BF16)
    w_qb2 = sb("w_qb2", [2, 128], BF16)
    w_kb2 = sb("w_kb2", [2, 128], BF16)
    w_ones2 = sb("w_ones2", [128, 2], BF16)
    w_gw = sb("w_gw", [128, 8, 8], BF16)
    w_gb = sb("w_gb", [8, 1], F32)
    w_b2 = sb("w_b2", [8, 1024], BF16)
    w_sel = sb("w_sel", [8, 8, 128], BF16)
    w_b1g = sb("w_b1g", [128, 64], F32)
    w_b1l = sb("w_b1l", [128, 64], F32)
    w_nw = sb("w_nw", [128, 8], F32)
    c_ident = sb("c_ident", [128, 128], BF16)
    c_ident32 = sb("c_ident32", [128, 128], F32)
    c_masks = sb("c_masks", [128, 896], BF16)
    c_iota = sb("c_iota", [128, 8], F32)
    c_iotaB = sb("c_iotaB", [128, 8], F32)
    c_ones1x128 = sb("c_ones1x128", [1, 128], F32)
    c_ones64 = sb("c_ones64", [64, 1], F32)
    mwork = sb("mwork", [64, 64], F32)
    mred = sb("mred", [64, 1], F32)
    mq_raw = sb("mq_raw", [1, 64], F32)
    mk_raw = sb("mk_raw", [1, 64], F32)
    msc = sb("msc", [1, 8], F32)
    negM = sb("negM", [128, 1], F32)
    sqs = sb("sqs", [128, 2, 1024], BF16)
    stat = sb("stat", [128, 2, 12], F32)
    epsA = sb("epsA", [128, 1], F32)
    eps2 = sb("eps2", [2, 1], F32)
    xhat = sb("xhat", [128, 2, 1024], BF16)
    lg_tm = sb("lg_tm", [128, NTILE, 8], F32)
    t2 = sb("t2", [128, 48], F32)
    comb_tm = sb("comb_tm", [128, 8], BF16)
    combT = sb("combT", [8, 2048], BF16)
    cqs = sb("cqs", [128, 2, 2, 512], BF16)
    qc_sb = sb("qc_sb", [128, 512], BF16)
    sq_sb = sb("sq_sb", [128, 512], BF16)
    rln = sb("rln", [2, 512], F32)
    rstd2r = sb("rstd2r", [2, 512], BF16)
    qcn_sb = sb("qcn_sb", [128, 512], BF16)
    qr_sb = sb("qr_sb", [128, 512], BF16)
    kvsb = sb("kvsb", [128, 2, 512], BF16)
    kcp = sb("kcp", [128, 512], BF16)
    vp = sb("vp", [128, 512], BF16)
    recip = sb("recip", [128, 4], F32)
    tp = sqs.ap().rearrange("p a n -> p a n")
    _xh_flat = xhat.ap().rearrange("p a n -> p (a n)")
    ctxsb = _xh_flat[0:65, 0:512]
    attst = _xh_flat[:, 1024:1536].rearrange("p (s k d) -> p s k d", s=2, k=4)
    sfin = sb("sfin", [128, 1024], BF16)
    xatt = sb("xatt", [128, 1024], BF16)
    ffin = sb("ffin", [128, 1024], F32)

    ps4 = st.enter_context(nc.psum_tensor("ps4", [128, 3072], F32))
    pc = st.enter_context(nc.psum_tensor("pc", [128, 512], F32))
    pbt = st.enter_context(nc.psum_tensor("pbt", [128, 1024], BF16))
    pb4 = ps4.ap()[:, 2048:2560]
    pb5 = ps4.ap()[:, 2560:3072]

    kv = kvq.ap()
    xnT = kv[:, 0:16384].rearrange("p (c t) -> p c t", c=8)
    xh2 = kv[:, 0:8192]
    lgT = kv[0:8, 16384:18432]
    combsb = kv[:, 38912:43008].rearrange("p (h t) -> p h t", h=8)
    besb = kv[:, 43008:43520]
    gsb = kv[:, 43520:44032]
    silu_sb = kv[:, 44032:44544]
    lsb = kv[:, 44544:45056]
    act_tmp = kv[:, 45056:45568]
    K2 = [kv[:, 8192:16384], kv[:, 16384:24576]]
    Q2 = [kv[:, 24576:32768], kv[:, 32768:40960]]
    V2 = [kv[:, 40960:45184].rearrange("p (a c) -> p a c", c=66),
          kv[:, 45184:49408].rearrange("p (a c) -> p a c", c=66)]
    ahat = kv[:, 0:32768].rearrange("p (e i t) -> p e i t", e=8, i=8)
    w1f = [kv[:, 1024 * i:1024 * (i + 1)].rearrange("p (c t) -> p c t", c=8)
           for i in range(4)]
    f_act = kv[:, 4096:4608]
    f_ah = [kv[:, 4608:5120], kv[:, 5120:5632]]
    f_be = kv[:, 6144:6656]
    f_g = kv[:, 6656:7168]
    f_si = kv[:, 7168:7680]
    f_l = kv[:, 7680:8192]
    _w1r2 = kv[:, 32768:36864].rearrange("p (r c t) -> p r c t", r=2, c=8)
    _w2r2 = kv[:, 36864:38912].rearrange("p (r c t) -> p r c t", r=2, c=8)
    w1slots = [_w1r2[:, 0], _w1r2[:, 1],
               kv[:, 46592:48640].rearrange("p (c t) -> p c t", c=8)]
    w2slots = [_w2r2[:, 0], _w2r2[:, 1],
               kv[:, 45568:46592].rearrange("p (c t) -> p c t", c=8),
               kv[:, 32768:33792].rearrange("p (c t) -> p c t", c=8),
               kv[:, 33792:34816].rearrange("p (c t) -> p c t", c=8),
               kv[:, 34816:35840].rearrange("p (c t) -> p c t", c=8)]
    Q0, Q1, Q2_, Q3 = (ps4.ap()[:, 0:512], ps4.ap()[:, 512:1024],
                       ps4.ap()[:, 1024:1536], ps4.ap()[:, 1536:2048])

    x_toks = []
    x_free = []

    def p0():
        plain = [(w_gw, gwT_in), (w_b2, b2_in), (w_sel, selE_in),
                 (w_nw, nw_cols), (c_ident, ident_in),
                 (c_ident32, ident32_in), (c_masks, masks_in),
                 (c_iota, iota8_in), (c_iotaB, iotaB_in),
                 (c_ones1x128, ones1x128), (c_ones64, ones64),
                 (mq_raw, qnw_row), (mk_raw, knw_row), (mwork, wqr_raw)]
        tw = None
        for dst, src in plain:
            tw = P.op("sync", lambda sy, d=dst, s=src:
                      sy.dma_start(out=d[...], in_=s[...]), n_dma=1,
                      chan="wload")
        cast = [(w_dq, wdqT), (w_uq, wuqT), (w_dkv, wdkvT), (w_ukv, wukvT),
                (w_qrbd, wqr_bd), (w_krbd, wkr_bd), (w_wobd, wo_bd),
                (w_qb2, qbcast2), (w_kb2, kbcast2), (w_ones2, ones2_in),
                (w_gb, gbias), (w_b1g, b1g_cols), (w_b1l, b1l_cols)]
        tc = None
        for dst, src in cast:
            tc = P.op("gpsimd", lambda g, d=dst, s=src:
                      g.dma_start(out=d[...], in_=s[...]), n_dma=1,
                      chan="gload")
        for t in range(2):
            b, ti = t // 8, t % 8
            x_toks.append(P.op(
                "sync",
                lambda sy, b=b, ti=ti, t=t: sy.dma_start(
                    out=xr[:, t % 2, :],
                    in_=x_own[b].rearrange("(n p) d -> n p d", p=128)[ti]),
                n_dma=1, chan="x%d" % (t % 2)))
        te = P.op("vector", lambda v: v.memset(epsA[:, :], LN_EPS))
        te2 = P.op("vector", lambda v: v.memset(eps2[:, :], RMS_EPS))
        tv1 = P.op("vector", lambda v: v.memset(V2[0][:, :, 64:65], 1.0))
        tv2 = P.op("vector", lambda v: v.memset(V2[1][:, :, 64:65], 1.0))
        ta = P.op("scalar", lambda s: s.activation(mwork[:, :], mwork[:, :],
                                                   AF.Square), waits=[tw, tc])
        tvr = P.op("vector", lambda v: v.reduce_sum(mred[:, 0:1], mwork[:, :],
                                                    axis=AX.X), waits=[ta])
        tt = P.op("tensor", lambda t: t.matmul(pc[0:1, 0:1], c_ones64[:, 0:1],
                                               mred[:, 0:1], start=True,
                                               stop=True), waits=[tvr])
        tv = P.op("vector", lambda v: v.tensor_copy(msc[0:1, 0:1], pc[0:1, 0:1]),
                  waits=[tt])
        tg = P.op("gpsimd", lambda g: g.dma_start(out=mwork[:, :], in_=wkr_raw),
                  waits=[tvr], n_dma=1, chan="gload")
        ta = P.op("scalar", lambda s: s.activation(mwork[:, :], mwork[:, :],
                                                   AF.Square), waits=[tg])
        tv2b = P.op("vector", lambda v: v.reduce_sum(mred[:, 0:1], mwork[:, :],
                                                     axis=AX.X), waits=[ta, tv])
        tt = P.op("tensor", lambda t: t.matmul(pc[0:1, 1:2], c_ones64[:, 0:1],
                                               mred[:, 0:1], start=True,
                                               stop=True), waits=[tv2b])
        seq = [
            lambda v: v.tensor_copy(msc[0:1, 1:2], pc[0:1, 1:2]),
            lambda v: v.tensor_mul(mwork[0:1, 0:64], mq_raw[0:1, :],
                                   mq_raw[0:1, :]),
            lambda v: v.reduce_max(msc[0:1, 2:3], mwork[0:1, 0:64], axis=AX.X),
            lambda v: v.tensor_mul(mwork[0:1, 0:64], mk_raw[0:1, :],
                                   mk_raw[0:1, :]),
            lambda v: v.reduce_max(msc[0:1, 3:4], mwork[0:1, 0:64], axis=AX.X),
            lambda v: v.tensor_scalar(msc[0:1, 4:5], msc[0:1, 0:1], 1.0, 64.0,
                                      ALU.add, ALU.mult),
            lambda v: v.tensor_mul(msc[0:1, 4:5], msc[0:1, 4:5], msc[0:1, 2:3]),
            lambda v: v.tensor_scalar_mul(msc[0:1, 5:6], msc[0:1, 1:2], 1024.0),
            lambda v: v.scalar_tensor_tensor(msc[0:1, 5:6], msc[0:1, 3:4], 64.0,
                                             msc[0:1, 5:6], ALU.mult, ALU.add),
            lambda v: v.tensor_mul(msc[0:1, 6:7], msc[0:1, 4:5], msc[0:1, 5:6]),
        ]
        tok = tt
        for fn in seq:
            tok = P.op("vector", fn, waits=[tok])
        tok = P.op("scalar", lambda s: s.activation(msc[0:1, 7:8], msc[0:1, 6:7],
                                                    AF.Ln), waits=[tok])
        tok = P.op("scalar", lambda s: s.activation(msc[0:1, 7:8], msc[0:1, 7:8],
                                                    AF.Exp, scale=0.5),
                   waits=[tok])
        tok = P.op("vector", lambda v: v.tensor_scalar(
            msc[0:1, 7:8], msc[0:1, 7:8], -0.125, -50.0,
            ALU.mult, ALU.max), waits=[tok])
        tok = P.op("tensor", lambda t: t.matmul(
            pc[:, 2:3], c_ones1x128[0:1, :], msc[0:1, 7:8],
            start=True, stop=True), waits=[tok])
        tok = P.op("vector", lambda v: v.tensor_copy(negM[:, 0:1], pc[:, 2:3]),
                   waits=[tok])
        return tok, te, te2, (tv1, tv2)

    negM_tok, eps_tok, eps2_tok, vones_toks = p0()

    xnT_toks = [None] * NTILE
    nfT_toks = [None] * NTILE
    xc_toks = [None] * NTILE

    def p1a():
        prev_nf = None
        prev_tf2 = None
        prev_te1 = None
        for t in range(NTILE):
            r = t % 2
            xt = xr.ap()[:, t % 2, :]
            sq = sqs.ap()[:, r, :]
            s_ = stat.ap()[:, r, :]
            xh = xhat.ap()[:, r, :]
            tv = P.op("vector", lambda v, xt=xt, s_=s_:
                      v.reduce_sum(s_[:, 0:1], xt, axis=AX.X),
                      waits=[x_toks[t], negM_tok])
            tv = P.op("vector", lambda v, xt=xt, sq=sq, s_=s_:
                      v.scalar_tensor_tensor(sq, xt, 1.0, xt, ALU.mult,
                                             ALU.mult,
                                             accum_out=s_[:, 1:2]),
                      waits=[tv])
            tv = P.op("vector", lambda v, s_=s_:
                      v.tensor_scalar_mul(s_[:, 2:3], s_[:, 0:1], 1.0 / 1024.0),
                      waits=[tv])
            tv = P.op("vector", lambda v, s_=s_:
                      v.tensor_mul(s_[:, 3:4], s_[:, 2:3], s_[:, 2:3]),
                      waits=[tv])
            tv = P.op("vector", lambda v, s_=s_:
                      v.scalar_tensor_tensor(s_[:, 4:5], s_[:, 1:2],
                                             1.0 / 1024.0, s_[:, 3:4],
                                             ALU.mult, ALU.subtract),
                      waits=[tv])
            ta = P.op("scalar", lambda s, s_=s_:
                      s.activation(s_[:, 8:9], s_[:, 4:5], AF.Ln,
                                   bias=epsA[:, 0:1]), waits=[tv, eps_tok])
            ta = P.op("scalar", lambda s, s_=s_:
                      s.activation(s_[:, 5:6], s_[:, 8:9], AF.Exp, scale=-0.5),
                      waits=[ta])
            tn = P.op("vector", lambda v, s_=s_: v.tensor_single_scalar(
                s_[:, 10:11], s_[:, 4:5], epsA[:, 0:1], ALU.add), waits=[ta])
            tn = P.op("vector", lambda v, s_=s_: v.tensor_mul(
                s_[:, 11:12], s_[:, 5:6], s_[:, 5:6]), waits=[tn])
            tn = P.op("vector", lambda v, s_=s_: v.tensor_mul(
                s_[:, 11:12], s_[:, 11:12], s_[:, 10:11]), waits=[tn])
            tn = P.op("vector", lambda v, s_=s_: v.tensor_scalar(
                s_[:, 11:12], s_[:, 11:12], -0.5, 1.5, ALU.mult, ALU.add),
                waits=[tn])
            tn = P.op("vector", lambda v, s_=s_: v.tensor_mul(
                s_[:, 5:6], s_[:, 5:6], s_[:, 11:12]), waits=[tn])
            txc = P.op("vector", lambda v, xt=xt, s_=s_, t=t:
                       v.tensor_scalar(xc[:, t, :], xt, s_[:, 2:3], s_[:, 5:6],
                                       ALU.subtract, ALU.mult), waits=[tn])
            xc_toks[t] = txc
            x_free.append(txc)
            tv = P.op("vector", lambda v, sq=sq, s_=s_, t=t:
                      v.scalar_tensor_tensor(sq, xc[:, t, :], 1.0,
                                             xc[:, t, :], ALU.mult, ALU.mult,
                                             accum_out=s_[:, 6:7]),
                      waits=[txc])
            ta = P.op("scalar", lambda s, s_=s_:
                      s.activation(s_[:, 9:10], s_[:, 6:7], AF.Ln,
                                   bias=epsA[:, 0:1], scale=1.0 / 1024.0),
                      waits=[tv])
            ta = P.op("scalar", lambda s, s_=s_:
                      s.activation(s_[:, 7:8], s_[:, 9:10], AF.Exp, scale=-0.5),
                      waits=[ta])
            tn = P.op("vector", lambda v, s_=s_: v.tensor_scalar(
                s_[:, 10:11], s_[:, 6:7], 1.0 / 1024.0, epsA[:, 0:1],
                ALU.mult, ALU.add), waits=[ta])
            tn = P.op("vector", lambda v, s_=s_: v.tensor_mul(
                s_[:, 11:12], s_[:, 7:8], s_[:, 7:8]), waits=[tn])
            tn = P.op("vector", lambda v, s_=s_: v.tensor_mul(
                s_[:, 11:12], s_[:, 11:12], s_[:, 10:11]), waits=[tn])
            tn = P.op("vector", lambda v, s_=s_: v.tensor_scalar(
                s_[:, 11:12], s_[:, 11:12], -0.5, 1.5, ALU.mult, ALU.add),
                waits=[tn])
            tn = P.op("vector", lambda v, s_=s_: v.tensor_mul(
                s_[:, 7:8], s_[:, 7:8], s_[:, 11:12]), waits=[tn])
            txh = P.op("vector", lambda v, s_=s_, t=t:
                       v.tensor_scalar_mul(ffin[:, :], xc[:, t, :],
                                           s_[:, 7:8]),
                       waits=[tn, prev_tf2])

            def mktrs(src):
                def f(tt):
                    last = None
                    for c in range(8):
                        last = tt.transpose(pbt[:, 128 * c:128 * (c + 1)],
                                            src[:, 128 * c:128 * (c + 1)],
                                            c_ident[:, :])
                    return last
                return f
            tf1 = P.op("tensor", mktrs(xc.ap()[:, t, :]),
                       waits=[txc, prev_te1])
            te1 = P.op("vector", lambda v, t=t: v.tensor_copy(
                xnT[:, :, 128 * t:128 * (t + 1)],
                pbt.ap().rearrange("p (c n) -> p c n", c=8)), waits=[tf1])
            xnT_toks[t] = te1
            prev_te1 = te1

            def f32trs(tt):
                last = None
                for c in range(8):
                    last = tt.transpose(ps4[:, 128 * c:128 * (c + 1)],
                                        ffin[:, 128 * c:128 * (c + 1)],
                                        c_ident32[:, :])
                return last
            tf2 = P.op("tensor", f32trs, waits=[txh, prev_nf])

            def nfev(s, t=t):
                last = None
                for c in range(8):
                    last = s.activation(nfT[:, c, 128 * t:128 * (t + 1)],
                                        ps4[:, 128 * c:128 * (c + 1)],
                                        AF.Copy, scale=w_nw[:, c:c + 1])
                return last
            te2 = P.op("scalar", nfev, waits=[tf2])
            nfT_toks[t] = te2
            prev_nf = te2
            prev_tf2 = tf2
            tn = t + 2
            if tn < NTILE:
                bn, tin = tn // 8, tn % 8
                x_toks.append(P.op(
                    "sync",
                    lambda sy, b=bn, ti=tin, t=tn: sy.dma_start(
                        out=xr[:, t % 2, :],
                        in_=x_own[b].rearrange("(n p) d -> n p d", p=128)[ti]),
                    waits=[txc], n_dma=1, chan="x%d" % (tn % 2)))

    p1a()

    def ex1():
        tok = None
        for c in range(8):
            tok = P.op("sync", lambda sy, c=c: sy.dma_start(
                out=ag1_in.rearrange("(c p) t -> c p t", p=128)[c],
                in_=xnT[:, c, :]), waits=xnT_toks, n_dma=1, chan="ag1send")
        return P.collective(lambda g: g.collective_compute(
            "AllGather", ALU.bypass, replica_groups=[list(range(NCORE))],
            ins=[ag1_in[:, :]], outs=[ag1_out[:, :]]), waits=[tok],
            chan="cc1")

    cc1_tok = ex1()

    def p1b():
        lgT_toks = []
        pxfree = negM_tok
        for tb in range(4):
            def gmm(t, tb=tb):
                last = None
                for c in range(8):
                    last = t.matmul(pc[0:8, :], w_gw[:, c, :],
                                    nfT[:, c, 512 * tb:512 * (tb + 1)],
                                    start=(c == 0), stop=(c == 7))
                return last
            tmm = P.op("tensor", gmm,
                       waits=[nfT_toks[4 * tb + i] for i in range(4)] + [pxfree])
            tev = P.op("vector", lambda v, tb=tb: v.tensor_single_scalar(
                lgT[:, 512 * tb:512 * (tb + 1)], pc[0:8, :], w_gb[:, 0:1],
                ALU.add), waits=[tmm])
            pxfree = tev
            lgT_toks.append(tev)
        lg_toks = []
        prev_ev = xnT_toks[-1]
        for tb in range(4):
            def ltr(t, tb=tb):
                last = None
                for k in range(4):
                    last = t.transpose(
                        pbt[:, 8 * k:8 * (k + 1)],
                        lgT[0:8, 512 * tb + 128 * k:512 * tb + 128 * (k + 1)],
                        c_ident[0:8, 0:8])
                return last
            tf = P.op("tensor", ltr, waits=[lgT_toks[tb], prev_ev])
            prev_ev = P.op("vector", lambda v, tb=tb: v.tensor_copy(
                lg_tm[:, 4 * tb:4 * (tb + 1), :],
                pbt.ap().rearrange("p (k e) -> p k e", k=128)[:, 0:4, :]),
                waits=[tf])
            lg_toks.append(prev_ev)
        t2a = t2.ap()
        m1, i1, m2_, i2 = (t2a[:, 0:1], t2a[:, 1:2], t2a[:, 2:3], t2a[:, 3:4])
        w2t, w1t = t2a[:, 4:5], t2a[:, 5:6]
        tt1 = t2a[:, 8:16]
        mi = t2a[:, 16:24]
        sel1 = t2a[:, 24:32]
        l2 = t2a[:, 32:40]
        sel2 = t2a[:, 40:48]
        prev_ctr = prev_ev
        for t in range(NTILE):
            lg = lg_tm.ap()[:, t, :]
            tok = P.op("vector", lambda v, lg=lg:
                       v.reduce_max(m1, lg, axis=AX.X),
                       waits=[lg_toks[t // 4], prev_ctr])
            chain = [
                lambda v, lg=lg: v.tensor_scalar(tt1, lg, m1, None, ALU.is_ge),
                lambda v: v.scalar_tensor_tensor(mi, tt1, -BIG, c_iotaB[:, :],
                                                 ALU.mult, ALU.add),
                lambda v: v.tensor_reduce(i1, mi, axis=AX.X, op=ALU.min),
                lambda v: v.tensor_scalar(sel1, c_iota[:, :], i1, None,
                                          ALU.is_equal),
                lambda v, lg=lg: v.scalar_tensor_tensor(l2, sel1, -BIG, lg,
                                                        ALU.mult, ALU.add),
                lambda v: v.reduce_max(m2_, l2, axis=AX.X),
                lambda v: v.tensor_sub(w2t, m2_, m1),
            ]
            for fn in chain:
                tok = P.op("vector", fn, waits=[tok])
            tok = P.op("scalar", lambda s: s.activation(w2t, w2t, AF.Sigmoid),
                       waits=[tok])
            chain2 = [
                lambda v: v.tensor_scalar(sel2, l2, m2_, None, ALU.is_ge),
                lambda v: v.scalar_tensor_tensor(mi, sel2, -BIG, c_iotaB[:, :],
                                                 ALU.mult, ALU.add),
                lambda v: v.tensor_reduce(i2, mi, axis=AX.X, op=ALU.min),
                lambda v: v.tensor_scalar(sel2, c_iota[:, :], i2, None,
                                          ALU.is_equal),
                lambda v: v.tensor_scalar(w1t, w2t, -1.0, 1.0, ALU.mult,
                                          ALU.add),
                lambda v: v.tensor_scalar_mul(comb_tm[:, :], sel1, w1t),
                lambda v: v.scalar_tensor_tensor(comb_tm[:, :], sel2, w2t,
                                                 comb_tm[:, :], ALU.mult,
                                                 ALU.add),
            ]
            for fn in chain2:
                tok = P.op("vector", fn, waits=[tok])
            tf = P.op("tensor", lambda t_: t_.transpose(
                pbt[0:8, 0:128], comb_tm[:, :], c_ident[:, :]), waits=[tok])
            prev_ctr = P.op("vector", lambda v, t=t: v.tensor_copy(
                combT[0:8, 128 * t:128 * (t + 1)], pbt[0:8, 0:128]),
                waits=[tf])
        return prev_ctr

    combT_tok = p1b()

    pid_sync = nc.partition_id(engines=[ET.SP])
    att_dma_toks = []
    xh2_readers_prev = []
    pbt_owner = combT_tok
    pcF = [combT_tok]
    b4F = [negM_tok]
    b5F = [negM_tok]
    flash_last_pe = None

    def p2_batch(b, flash_done_prev):
        nonlocal pbt_owner, flash_last_pe
        ag1v = ag1_out.rearrange("(r q p) t -> q p r t", r=NCORE, q=8)
        xh2v = xh2.rearrange("p (r n) -> p r n", r=NCORE)
        wv = [cc1_tok] + xh2_readers_prev + ([flash_done_prev] if flash_done_prev else [])
        xh2_ld = P.op("sync", lambda sy: sy.dma_start(
            out=xh2v,
            in_=ag1v[pid_sync, :, :, 1024 * b:1024 * (b + 1)]),
            waits=wv, n_dma=1, chan="xh2")
        xh2_rd = []
        q_toks = [None] * 16
        k_toks = [None] * 16
        prevq = [combT_tok if b == 0 else flash_done_prev]
        pend = []

        def flush_p2():
            while pend:
                pend.pop(0)()

        def waves(nch, wmat, accA_bank, accB_bank, accW, accA_waits,
                  accB_waits, ts_):
            ev_toks = [None, None]
            acc_last = [None]
            for c in range(nch):
                pg = c % 2
                qa = ps4.ap()[:, 512 * pg:512 * (pg + 1)]
                qb_ = ps4.ap()[:, 1024 + 512 * pg:1024 + 512 * (pg + 1)]
                wv = [xh2_ld]
                if c >= 2:
                    wv.append(ev_toks[pg])
                else:
                    wv.extend(prevq)
                P.op("tensor", lambda t, c=c, qa=qa, ts_=ts_, wmat=wmat:
                     t.matmul(qa, wmat[0:64, 128 * c:128 * (c + 1)],
                              xh2[0:64, ts_], start=True, stop=True),
                     waits=wv, token=False)
                tB = P.op("tensor", lambda t, c=c, qb_=qb_, ts_=ts_, wmat=wmat:
                          t.matmul(qb_, wmat[64:128, 128 * c:128 * (c + 1)],
                                   xh2[64:128, ts_], start=True, stop=True))
                xh2_rd.append(tB)
                if c == 2:
                    flush_p2()
                tevA = P.op("scalar", lambda sc, qa=qa, pg=pg:
                            sc.activation(cqs[:, 0, pg, :], qa, AF.Copy),
                            waits=[tB])
                tevB = P.op("vector", lambda v, qb_=qb_, pg=pg:
                            v.tensor_copy(cqs[:, 1, pg, :], qb_),
                            waits=[tB, tevA])
                ev_toks[pg] = tevB
                wA = [tevB] + (accA_waits if c == 0 else [])
                P.op("tensor", lambda t, c=c, pg=pg, accA_bank=accA_bank,
                     accW=accW: t.matmul(accA_bank, accW[:, c, :],
                                         cqs[:, 0, pg, :],
                                         start=(c == 0), stop=(c == nch - 1)),
                     waits=wA, token=False)
                wB = accB_waits if c == 0 else []
                acc_last[0] = P.op(
                    "tensor", lambda t, c=c, pg=pg, accB_bank=accB_bank,
                    accW=accW: t.matmul(accB_bank, accW[:, c, :],
                                        cqs[:, 1, pg, :],
                                        start=(c == 0), stop=(c == nch - 1)))
            return ev_toks, acc_last[0]

        for tb in range(16):
            ts_ = slice(512 * tb, 512 * (tb + 1))
            ev_toks, tacc = waves(4, w_dq, pb4[0:64, :], pb5[0:64, :], w_uq,
                                  [b4F[0]], [b5F[0]], ts_)
            tqc = P.op("scalar", lambda sc: sc.activation(
                qc_sb[0:64, :], pb4[0:64, :], AF.Copy), waits=[tacc])
            tqcB = P.op("vector", lambda v: v.tensor_copy(
                qc_sb[64:128, :], pb5[0:64, :]), waits=[tacc, tqc])

            def bundle_q(tb=tb, ts_=ts_, tqc=tqc, tqcB=tqcB):
                tsq = P.op("scalar", lambda s: s.activation(
                    sq_sb[:, :], qc_sb[:, :], AF.Square), waits=[tqcB])
                tss = P.op("tensor", lambda t: t.matmul(
                    pc[0:2, :], w_ones2[:, :], sq_sb[:, :],
                    start=True, stop=True), waits=[tsq, pcF[0], eps2_tok])
                tln = P.op("scalar", lambda s: s.activation(
                    rln[:, :], pc[0:2, :], AF.Ln, bias=eps2[:, 0:1],
                    scale=1.0 / 64.0), waits=[tss])
                trs = P.op("scalar", lambda s: s.activation(
                    rstd2r[:, :], rln[:, :], AF.Exp, scale=-0.5), waits=[tln])
                tb2 = P.op("tensor", lambda t: t.matmul(
                    pc[:, :], w_qb2[:, :], rstd2r[:, :], start=True,
                    stop=True), waits=[trs])
                tqcn = P.op("vector", lambda v: v.tensor_mul(
                    qcn_sb[:, :], pc[:, :], qc_sb[:, :]), waits=[tb2])
                tqr = P.op("tensor", lambda t: t.matmul(
                    pc[:, :], w_qrbd[:, :], qcn_sb[:, :], start=True,
                    stop=True), waits=[tqcn])
                tqrev = P.op("vector", lambda v: v.tensor_copy(
                    qr_sb[:, :], pc[:, :]), waits=[tqr])
                pcF[0] = tqrev
                P.op("vector", lambda v, ts_=ts_: v.tensor_copy(
                    Q2[0][0:64, ts_], qcn_sb[0:64, :]), waits=[tqcn],
                    token=False)
                P.op("vector", lambda v, ts_=ts_: v.tensor_copy(
                    Q2[0][64:128, ts_], qr_sb[0:64, :]), waits=[tqrev],
                    token=False)
                P.op("vector", lambda v, ts_=ts_: v.tensor_copy(
                    Q2[1][0:64, ts_], qcn_sb[64:128, :]), waits=[tqcn],
                    token=False)
                tq4 = P.op("vector", lambda v, ts_=ts_: v.tensor_copy(
                    Q2[1][64:128, ts_], qr_sb[64:128, :]), waits=[tqrev])
                q_toks[tb] = tq4
            pend.append(bundle_q)
            ev_toks, tacc = waves(8, w_dkv, pb5, pb4, w_ukv,
                                  [tqcB], [tqc], ts_)
            prevq = list(ev_toks)
            tkv0 = P.op("scalar", lambda sc: sc.activation(
                kvsb[:, 0, :], pb5[:, :], AF.Copy), waits=[tacc])
            tkv1 = P.op("vector", lambda v: v.tensor_copy(
                kvsb[:, 1, :], pb4[:, :]), waits=[tacc, tkv0])
            b5F[0] = tkv0
            b4F[0] = tkv1
            tk1 = P.op("vector", lambda v: v.tensor_copy(
                kcp[0:64, :], kvsb[0:64, 0, :]), waits=[tkv0])
            tk2 = P.op("vector", lambda v: v.tensor_copy(
                kcp[64:128, :], kvsb[0:64, 1, :]), waits=[tkv1, tk1])
            tv1 = P.op("vector", lambda v: v.tensor_copy(
                vp[0:64, :], kvsb[64:128, 0, :]), waits=[tk2])
            tv2 = P.op("vector", lambda v: v.tensor_copy(
                vp[64:128, :], kvsb[64:128, 1, :]), waits=[tv1])

            def bundle_k(tb=tb, ts_=ts_, tk2=tk2, tv2=tv2):
                tsq = P.op("scalar", lambda s: s.activation(
                    sq_sb[:, :], kcp[:, :], AF.Square), waits=[tk2])
                tss = P.op("tensor", lambda t: t.matmul(
                    pc[0:2, :], w_ones2[:, :], sq_sb[:, :],
                    start=True, stop=True), waits=[tsq, pcF[0]])
                tln = P.op("scalar", lambda s: s.activation(
                    rln[:, :], pc[0:2, :], AF.Ln, bias=eps2[:, 0:1],
                    scale=1.0 / 64.0), waits=[tss])
                trs = P.op("scalar", lambda s: s.activation(
                    rstd2r[:, :], rln[:, :], AF.Exp, scale=-0.5), waits=[tln])
                tb2 = P.op("tensor", lambda t: t.matmul(
                    pc[:, :], w_kb2[:, :], rstd2r[:, :], start=True,
                    stop=True), waits=[trs])
                tkcn = P.op("vector", lambda v: v.tensor_mul(
                    qcn_sb[:, :], pc[:, :], kcp[:, :]), waits=[tb2])
                tkr = P.op("tensor", lambda t, ts_=ts_: t.matmul(
                    pc[:, :], w_krbd[:, :], xh2[:, ts_], start=True,
                    stop=True), waits=[tkcn])
                xh2_rd.append(tkr)
                tkrev = P.op("vector", lambda v: v.tensor_copy(
                    qr_sb[:, :], pc[:, :]), waits=[tkr])
                P.op("vector", lambda v, ts_=ts_: v.tensor_copy(
                    K2[0][0:64, ts_], qcn_sb[0:64, :]), waits=[tkcn],
                    token=False)
                P.op("vector", lambda v, ts_=ts_: v.tensor_copy(
                    K2[0][64:128, ts_], qr_sb[0:64, :]), waits=[tkrev],
                    token=False)
                P.op("vector", lambda v, ts_=ts_: v.tensor_copy(
                    K2[1][0:64, ts_], qcn_sb[64:128, :]), waits=[tkcn],
                    token=False)
                tk4 = P.op("vector", lambda v, ts_=ts_: v.tensor_copy(
                    K2[1][64:128, ts_], qr_sb[64:128, :]), waits=[tkrev])
                tvw = P.op("tensor", lambda t: t.matmul(
                    pc[:, :], w_wobd[:, :], vp[:, :], start=True, stop=True),
                    waits=[tv2, tkrev])
                tvwe = P.op("vector", lambda v: v.tensor_copy(
                    vp[:, :], pc[:, :]), waits=[tvw])
                pcF[0] = tvwe

                def vtrs(t):
                    last = None
                    for k in range(4):
                        last = t.transpose(pbt[:, 128 * k:128 * (k + 1)],
                                           vp[:, 128 * k:128 * (k + 1)],
                                           c_ident[:, :])
                    return last
                tvf = P.op("tensor", vtrs, waits=[tvwe, pbt_owner])
                pv = pbt.ap()[:, 0:512].rearrange("p (k h) -> p k h", k=4)
                tve1 = P.op("vector", lambda v, tb=tb: v.tensor_copy(
                    V2[0][:, 4 * tb:4 * (tb + 1), 0:64], pv[:, :, 0:64]),
                    waits=[tvf] + (list(vones_toks)
                                   if tb == 0 and b == 0 else []))
                tve2 = P.op("vector", lambda v, tb=tb: v.tensor_copy(
                    V2[1][:, 4 * tb:4 * (tb + 1), 0:64], pv[:, :, 64:128]),
                    waits=[tve1])
                nonlocal_pbt[0] = tve2
                k_toks[tb] = (tk4, tve2)
            nonlocal_pbt = [pbt_owner]
            pend.append(bundle_k)
            def _sync_pbt(nl=nonlocal_pbt):
                nonlocal pbt_owner
                pbt_owner = nl[0]
            pend.append(_sync_pbt)
        flush_p2()
        return xh2_ld, q_toks, k_toks, xh2_rd

    class FFNFiller:

        def __init__(self):
            self.steps = []
            self.pcf = [None]
            self.prev_tbev = [None]
            self.prev_tg1 = [None]
            self.prev_tg2 = [None]
            self.prev_tsi = [None]
            self.ah_dma = {}
            self.nslot = [0]
            self.tg_tok = [None]
            self.slices = []
            self.kcons = [0]
            self.slice_tok = {}

        def _load_slice(self, k, waits):
            if k >= len(self.slices):
                return
            e, co = self.slices[k]
            sl = k % 4
            self.slice_tok[k] = P.op(
                "sync", lambda sy, e=e, co=co, sl=sl: sy.dma_start(
                    out=w1f[sl][...],
                    in_=w1pT[e].rearrange("(h p) o -> p h o", p=128)[
                        :, :, co:co + 128]),
                waits=waits, n_dma=1, chan="w1f%d" % sl, token=True)

        def build(self, tbs, first_wait):
            self.pcf[0] = pcF[0]
            for tb in tbs:
                ts_ = slice(512 * tb, 512 * (tb + 1))
                for e in range(8):
                    self.steps.append(self._mk_be(tb, e, ts_))
                    for c in range(8):
                        self.slices.append((e, 256 * c))
                        self.slices.append((e, 256 * c + 128))
                        self.steps.append(self._mk_mm(tb, e, c, ts_, False))
                        self.steps.append(self._mk_mm(tb, e, c, ts_, True))
            for k in range(4):
                self._load_slice(k, [first_wait])

        def _mk_be(self, tb, e, ts_):
            def f():
                tbe = P.op("tensor", lambda t, e=e, ts_=ts_: t.matmul(
                    pc[:, :], w_sel[:, e, :], combT[0:8, ts_],
                    start=True, stop=True),
                    waits=[self.pcf[0], self.prev_tg2[0], combT_tok])
                tbev = P.op("scalar", lambda s: s.activation(
                    f_be[:, :], pc[:, :], AF.Copy, scale=1.0 / 1.702),
                    waits=[tbe, self.prev_tg2[0]])
                self.pcf[0] = tbev
                self.prev_tbev[0] = tbev
            return f

        def _mk_mm(self, tb, e, c, ts_, is_lin):
            def f():
                k = self.kcons[0]
                sl = k % 4

                def mmg(t, sl=sl, ts_=ts_):
                    last = None
                    for hc in range(8):
                        last = t.matmul(pc[:, :], w1f[sl][:, hc, :],
                                        nfT[:, hc, ts_], start=(hc == 0),
                                        stop=(hc == 7))
                    return last
                tmm = P.op("tensor", mmg,
                           waits=[self.slice_tok[k], self.pcf[0]])
                self.kcons[0] += 1
                self._load_slice(k + 4, [tmm])
                bcol = 8 * e + c
                if not is_lin:
                    tg = P.op("vector", lambda v, bcol=bcol: v.tensor_scalar(
                        f_g[:, :], pc[:, :], w_b1g[:, bcol:bcol + 1], 7.0,
                        ALU.add, ALU.min), waits=[tmm, self.prev_tsi[0]])
                    self.pcf[0] = tg
                    self.tg_tok[0] = tg
                else:
                    tl = P.op("vector", lambda v, bcol=bcol: v.tensor_scalar(
                        f_l[:, :], pc[:, :], w_b1l[:, bcol:bcol + 1], 7.0,
                        ALU.add, ALU.min), waits=[tmm, self.prev_tg1[0]])
                    self.pcf[0] = tl
                    tl = P.op("vector", lambda v: v.tensor_scalar(
                        f_l[:, :], f_l[:, :], -7.0, 1.0, ALU.max, ALU.add),
                        waits=[tl])
                    tsi = P.op("scalar", lambda s: s.activation(
                        f_si[:, :], f_g[:, :], AF.Silu, scale=1.702),
                        waits=[self.tg_tok[0], self.prev_tg1[0]])
                    self.prev_tsi[0] = tsi
                    tg1 = P.op("gpsimd", lambda g: g.tensor_mul(
                        f_act[:, :], f_si[:, :], f_l[:, :]),
                        waits=[tsi, tl, self.prev_tg2[0]])
                    self.prev_tg1[0] = tg1
                    asl = self.nslot[0] % 2
                    tg2 = P.op("gpsimd", lambda g, asl=asl: g.tensor_mul(
                        f_ah[asl][...], f_act[:, :], f_be[:, :]),
                        waits=[tg1, self.prev_tbev[0],
                               self.ah_dma.get(self.nslot[0] - 2)])
                    self.prev_tg2[0] = tg2
                    tdma = P.op("sync", lambda sy, tb=tb, e=e, c=c, asl=asl:
                                sy.dma_start(
                                    out=ahat_d[tb, e, c],
                                    in_=f_ah[asl][...]),
                                waits=[tg2], n_dma=1, chan="ahf%d" % asl)
                    self.ah_dma[self.nslot[0]] = tdma
                    self.nslot[0] += 1
            return f

        def step(self):
            if self.steps:
                self.steps.pop(0)()

        def drain(self):
            while self.steps:
                self.steps.pop(0)()

    def flash_batch(b, q_toks, k_toks, filler):
        nonlocal pbt_owner, flash_last_pe
        expT = [{}, {}]
        pvT = [{}, {}]
        gH = [0]
        ctx_ev_h = {0: b4F[0], 1: b5F[0]}
        wo_ev = [pcF[0]]
        ctxsb_free = [pcF[0]]
        div_tok = [None]
        last_tf = [None]
        nepi = [0]
        pending = []
        Sslot = [ps4.ap()[:, 0:1024], ps4.ap()[:, 1024:2048]]
        Cslot = [pb4, pb5]

        def emit_epilogue(hh, qb, pv_last):
            ctx_ev = P.op("vector", lambda v, hh=hh: v.tensor_copy(
                ctxsb[:, :], Cslot[hh][0:65, :]),
                waits=[pv_last, ctxsb_free[0]])
            ctx_ev_h[hh] = ctx_ev
            sl = nepi[0] % 2
            nepi[0] += 1

            def otrs(t, sl=sl):
                last = None
                for k in range(4):
                    last = t.transpose(pbt[:, 512 * sl + 68 * k:
                                           512 * sl + 68 * k + 65],
                                       ctxsb[0:65, 128 * k:128 * (k + 1)],
                                       c_ident[0:65, 0:65])
                return last
            tf = P.op("tensor", otrs, waits=[ctx_ev, pbt_owner, div_tok[0]])
            ctxsb_free[0] = tf
            last_tf[0] = tf
            dtok = tf
            slot_guard = att_dma_toks[-2] if len(att_dma_toks) >= 2 else None
            for k in range(4):
                dtok = P.op("vector", lambda v, k=k, sl=sl: v.reciprocal(
                    recip[:, k:k + 1],
                    pbt[:, 512 * sl + 68 * k + 64:512 * sl + 68 * k + 65]),
                    waits=[dtok, slot_guard] if k == 0 else [dtok])
                dtok = P.op("vector", lambda v, k=k, sl=sl: v.tensor_scalar_mul(
                    attst[:, sl, k, :],
                    pbt[:, 512 * sl + 68 * k:512 * sl + 68 * k + 64],
                    recip[:, k:k + 1]), waits=[dtok])
            div_tok[0] = dtok
            rowv = ag2_in.rearrange("(n k p) c -> n k p c", k=4, p=128)
            tdma = P.op("sync", lambda sy, b=b, hh=hh, qb=qb, sl=sl:
                        sy.dma_start(
                            out=rowv[16 * b + qb, :, :,
                                     64 * hh:64 * (hh + 1)]
                            .rearrange("k p c -> p k c"),
                            in_=attst[:, sl, :, :]),
                        waits=[dtok], n_dma=1, chan="attst%d" % sl)
            att_dma_toks.append(tdma)
            return dtok

        def flush_pending():
            nonlocal pbt_owner
            while pending:
                args = pending.pop(0)
                pbt_owner = emit_epilogue(*args)

        ready = [q_toks[-1], k_toks[-1][0], k_toks[-1][1]]
        for qb in range(16):
            for j in range(qb + 1):
                for half in range(2):
                    g = gH[0]
                    Stoks = [None, None]
                    for hh in range(2):
                        wv = list(ready)
                        ready = []
                        if expT[hh].get(g - 1) is not None:
                            wv.append(expT[hh][g - 1])
                        for kt2 in range(2):
                            kt = 2 * half + kt2
                            kg = 4 * j + kt
                            tok = P.op(
                                "tensor",
                                lambda t, hh=hh, qb=qb, kg=kg, kt2=kt2:
                                t.matmul(
                                    Sslot[hh][:, 512 * kt2:512 * (kt2 + 1)],
                                    K2[hh][:, 128 * kg:128 * (kg + 1)],
                                    Q2[hh][:, 512 * qb:512 * (qb + 1)],
                                    start=True, stop=True),
                                waits=wv, token=(kt2 == 1))
                            wv = []
                        Stoks[hh] = tok
                    if j == 0 and half == 0:
                        flush_pending()
                    for hh in range(2):
                        wv = [Stoks[hh], negM_tok]
                        if pvT[hh].get(g - 1) is not None:
                            wv.append(pvT[hh][g - 1])
                        texp = P.op("scalar", lambda s, hh=hh: s.activation(
                            tp[:, hh, :], Sslot[hh][:, :],
                            AF.Exp, bias=negM[:, 0:1], scale=0.125),
                            waits=wv)
                        expT[hh][g] = texp
                        if j == qb:
                            for kt2 in range(2):
                                kt = 2 * half + kt2
                                texp = P.op("vector", lambda v, hh=hh, kt=kt,
                                            kt2=kt2: v.tensor_mul(
                                                tp[:, hh, 512 * kt2:
                                                   512 * (kt2 + 1)],
                                                tp[:, hh, 512 * kt2:
                                                   512 * (kt2 + 1)],
                                                c_masks[:, 384 - 128 * kt:
                                                        896 - 128 * kt]),
                                            waits=[texp])
                            expT[hh][g] = texp
                    filler.step()
                    for hh in range(2):
                        wpv = [expT[hh][g]]
                        if j == 0 and half == 0:
                            wpv.append(ctx_ev_h[hh])
                        for kt2 in range(2):
                            kt = 2 * half + kt2
                            kg = 4 * j + kt
                            tpv = P.op(
                                "tensor",
                                lambda t, hh=hh, kg=kg, kt2=kt2,
                                st_=(j == 0 and half == 0 and kt2 == 0),
                                sp_=(j == qb and kt == 3): t.matmul(
                                    Cslot[hh][0:65, :],
                                    V2[hh][:, kg, 0:65],
                                    tp[:, hh, 512 * kt2:512 * (kt2 + 1)],
                                    start=st_, stop=sp_),
                                waits=wpv, token=(kt2 == 1))
                            wpv = []
                        pvT[hh][g] = tpv
                    gH[0] += 1
            pending.append((0, qb, pvT[0][gH[0] - 1]))
            pending.append((1, qb, pvT[1][gH[0] - 1]))
        flush_pending()
        filler.drain()
        b4F[0] = ctx_ev_h[0]
        b5F[0] = ctx_ev_h[1]
        flash_last_pe = last_tf[0]
        return flash_last_pe

    fd = None
    ah_done = []
    for b in range(B):
        xh2_ld, q_toks, k_toks, xh2_rd = p2_batch(b, fd)
        xh2_readers_prev.clear()
        xh2_readers_prev.extend([xh2_rd[-1]])
        filler = FFNFiller()
        filler.build([2 * b, 2 * b + 1], xh2_rd[-1])
        fd = flash_batch(b, q_toks, k_toks, filler)
        xh2_readers_prev.append(filler.ah_dma[filler.nslot[0] - 1])
        xh2_readers_prev.append(filler.ah_dma[filler.nslot[0] - 2])
        pcF[0] = filler.pcf[0]
        ah_done.append(filler.ah_dma[filler.nslot[0] - 1])
        ah_done.append(filler.ah_dma[filler.nslot[0] - 2])

    cc2_tok = P.collective(lambda g: g.collective_compute(
        "AllGather", ALU.bypass, replica_groups=[list(range(NCORE))],
        ins=[ag2_in[:, :]], outs=[ag2_out[:, :]]), waits=att_dma_toks,
        chan="cc2")

    pid_g = nc.partition_id(engines=[ET.Pool])

    def p4():
        nonlocal pbt_owner
        w2_free = [fd] * 6
        pcpw2 = [pcF[0], b5F[0]]
        comb_toks = [None] * 8
        fin_prev = []
        out_toks = []
        prev_p2_last = [fd]
        for tb in range(4):
            ts_ = slice(512 * tb, 512 * (tb + 1))
            ah_toks = []
            for e in range(8):
                wv_ah = ([prev_p2_last[0]] + ah_done if e < 4
                         else [ah_toks[e - 4]])
                ah_toks.append(P.op(
                    "sync", lambda sy, tb=tb, e=e: sy.dma_start(
                        out=ahat[:, e, :, :],
                        in_=ahat_d[tb, e].rearrange("c p t -> p c t")),
                    waits=wv_ah, n_dma=1, chan="ahin%d" % (e % 4)))
            for hc in range(8):
                pg = hc % 2
                dst = pc if pg == 0 else None
                wv = [combT_tok, pcpw2[pg]]
                dsta = pc[:, :] if pg == 0 else pb5[:, :]
                tbias = P.op("tensor", lambda t, hc=hc, dsta=dsta, ts_=ts_:
                             t.matmul(dsta, w_b2[0:8, 128 * hc:
                                               128 * (hc + 1)],
                                      combT[0:8, ts_], start=True, stop=False),
                             waits=wv, token=False)
                last = None
                for e in range(8):
                    sl = (8 * hc + e) % 6
                    tld = P.op("sync", lambda sy, e=e, hc=hc, sl=sl:
                               sy.dma_start(
                                   out=w2slots[sl][...],
                                   in_=w2T[e].rearrange(
                                       "(i p) h -> p i h", p=128)[
                                       :, :, 128 * hc:128 * (hc + 1)]),
                               waits=[w2_free[sl]], n_dma=1,
                               chan="w2r%d" % sl)

                    def p2mm(t, e=e, sl=sl, dsta=dsta, last_e=(e == 7)):
                        last = None
                        for ic in range(8):
                            last = t.matmul(dsta, w2slots[sl][:, ic, :],
                                            ahat[:, e, ic, :], start=False,
                                            stop=(last_e and ic == 7))
                        return last
                    last = P.op("tensor", p2mm, waits=[tld, ah_toks[e]])
                    w2_free[sl] = last
                if pg == 0:
                    tcev = P.op("scalar", lambda s, hc=hc, dsta=dsta:
                                s.activation(combsb[:, hc, :], dsta, AF.Copy),
                                waits=[last])
                else:
                    tcev = P.op("vector", lambda v, hc=hc, dsta=dsta:
                                v.tensor_copy(combsb[:, hc, :], dsta),
                                waits=[last])
                pcpw2[pg] = tcev
                comb_toks[hc] = tcev
                prev_p2_last[0] = last
            for u in range(4):
                t_ = 4 * tb + u
                b, ti = t_ // 8, t_ % 8

                def ctrs(t, u=u):
                    last = None
                    for hc in range(8):
                        last = t.transpose(pbt[:, 128 * hc:128 * (hc + 1)],
                                           combsb[:, hc, 128 * u:
                                                  128 * (u + 1)],
                                           c_ident[:, :])
                    return last
                tf = P.op("tensor", ctrs, waits=comb_toks + [pbt_owner])
                wsf = [tf] + ([fin_prev[-1]] if fin_prev else [])
                ts1 = P.op("vector", lambda v, t_=t_: v.scalar_tensor_tensor(
                    sfin[:, 0:512], pbt[:, 0:512], 1.0,
                    xc[:, t_, 0:512], ALU.mult, ALU.add), waits=wsf)
                ts2 = P.op("vector", lambda v, t_=t_: v.scalar_tensor_tensor(
                    sfin[:, 512:1024], pbt[:, 512:1024], 1.0,
                    xc[:, t_, 512:1024], ALU.mult, ALU.add), waits=[ts1])
                pbt_owner = ts2
                sl = t_ % 2
                wx = [fin_prev[t_ - 2]] if t_ >= 2 else [xc_toks[14 + t_], fd]
                txl = P.op("sync", lambda sy, b=b, ti=ti, t_=t_:
                           sy.dma_start(out=xr[:, t_ % 2, :],
                                        in_=x_own[b].rearrange(
                                            "(n p) d -> n p d", p=128)[ti]),
                           waits=wx, n_dma=1, chan="x%d" % (t_ % 2))
                ag2v = ag2_out.rearrange("(r b2 q u p) c -> q b2 u p r c",
                                         r=NCORE, b2=B, q=8, u=8)
                wa = [cc2_tok] + ([fin_prev[t_ - 1]] if t_ >= 1 else [])
                tal = P.op("gpsimd", lambda g, b=b, ti=ti:
                           g.dma_start(
                               out=xatt.ap().rearrange(
                                   "p (r c) -> p r c", r=8),
                               in_=ag2v[pid_g, b, ti]),
                           waits=wa, n_dma=1, chan="ag2g")
                wf = [txl, tal] + ([out_toks[t_ - 1]] if t_ >= 1 else [])
                tff = P.op("vector", lambda v, t_=t_: v.tensor_add(
                    ffin[:, :], xr[:, t_ % 2, :], xatt[:, :]),
                    waits=wf)
                tff = P.op("vector", lambda v: v.tensor_add(
                    ffin[:, :], ffin[:, :], sfin[:, :]),
                    waits=[tff, ts2])
                fin_prev.append(tff)
                tout = P.op("sync", lambda sy, b=b, ti=ti:
                            sy.dma_start(
                                out=out_own[b].rearrange(
                                    "(n p) d -> n p d", p=128)[ti],
                                in_=ffin[:, :]), waits=[tff], n_dma=1,
                            chan="outd")
                out_toks.append(tout)
        return out_toks

    if stop_after != "flash":
        out_toks = p4()
        P.op("sync", lambda sy: sy.nop(), waits=[out_toks[-1]], token=False)

    sem_handles = {}
    with st:
        for e in ENGS:
            sem_handles[e] = st.enter_context(nc.semaphore("sem_" + e))
        for ch in sorted(P.chans):
            sem_handles[ch] = st.enter_context(nc.semaphore("sem_" + ch))
        with nc.Block() as block:
            P.emit(block, sem_handles)
    return nc


def kernel(**inputs):
    nc = build_program()
    in_maps = [_host_inputs(inputs, c) for c in range(NCORE)]
    res = run_bass_kernel_spmd(nc, in_maps, core_ids=list(range(NCORE)))
    out = np.empty((B, TSEQ, 1024), np.float32)
    for c in range(NCORE):
        o = res.results[c]["out_own"]
        for b in range(B):
            out[b, TOKC * c:TOKC * (c + 1)] = o[b]
    return out


# version 15
# speedup vs baseline: 1.0086x; 1.0086x over previous
import numpy as np
import ml_dtypes
from contextlib import ExitStack

import concourse.bass as bass
import concourse.mybir as mybir
from concourse.bass_utils import run_bass_kernel_spmd

F32 = mybir.dt.float32
BF16 = mybir.dt.bfloat16
AF = mybir.ActivationFunctionType
ALU = mybir.AluOpType
AX = mybir.AxisListType
ET = mybir.EngineType

NCORE = 8
B = 2
TSEQ = 8192
TOKC = 1024
NTILE = 16
LN_EPS = 1e-5
RMS_EPS = 1.1920929e-07
BIG = 1.0e6

bf16 = ml_dtypes.bfloat16
ENGS = ("sync", "tensor", "scalar", "vector", "gpsimd")
ENG_ET = {"sync": ET.SP, "tensor": ET.PE, "scalar": ET.Activation,
          "vector": ET.DVE, "gpsimd": ET.Pool}


class Planner:

    def __init__(self):
        self.segments = [{e: [] for e in ENGS}]
        self.count = {}
        self.chans = set()

    @property
    def ops(self):
        return self.segments[-1]

    def op(self, eng, fn, waits=(), token=True, n_dma=0, chan=None):
        if n_dma:
            assert chan is not None, "DMA op needs a channel"
            key = chan
            self.chans.add(chan)
            k = 16 * n_dma
        else:
            key = eng
            k = 1 if token else 0
        tok = None
        if k:
            self.count[key] = self.count.get(key, 0) + k
            tok = (key, self.count[key])
        self.ops[eng].append((fn, tuple(w for w in waits if w is not None), k,
                              key))
        return tok

    def collective(self, fn, waits=(), chan=None):
        self.chans.add(chan)
        self.count[chan] = self.count.get(chan, 0) + 1
        tok = (chan, self.count[chan])
        self.segments.append({e: [] for e in ENGS})
        self.ops["gpsimd"].append(
            (fn, tuple(w for w in waits if w is not None), 1, chan))
        self.segments.append({e: [] for e in ENGS})
        return tok

    def emit(self, block, sems):
        for seg in self.segments:
            for eng in ENGS:
                ops = seg[eng]
                if not ops:
                    continue

                def body(h, _ops=ops, _eng=eng):
                    for fn, waits, k, key in _ops:
                        for (we, wv) in waits:
                            h.wait_ge(sems[we], wv)
                        instr = fn(h)
                        if k:
                            assert instr is not None, \
                                f"op on {_eng} returned None"
                            instr.then_inc(sems[key], k)
                getattr(block, eng)(body)


def _host_inputs(inp, core):
    x = inp["L0_attn_x"]
    d = {}
    d["x_own"] = np.ascontiguousarray(
        np.stack([x[b, TOKC * core:TOKC * (core + 1)] for b in range(B)]))
    d["wdqT"] = np.ascontiguousarray(
        np.concatenate([inp["L0_attn_W_DQ"].T, inp["L0_attn_W_DQ"].T], axis=0))
    d["wuqT"] = np.ascontiguousarray(
        inp["L0_attn_W_UQ"].T.reshape(4, 128, 64).transpose(1, 0, 2))
    d["wdkvT"] = np.ascontiguousarray(
        np.concatenate([inp["L0_attn_W_DKV"].T, inp["L0_attn_W_DKV"].T], axis=0))
    d["wukvT"] = np.ascontiguousarray(
        inp["L0_attn_W_UKV"].T.reshape(8, 128, 128).transpose(1, 0, 2))
    wqr = inp["L0_attn_W_QR"]
    wkr = inp["L0_attn_W_KR"]
    z = np.zeros((128, 128), np.float32)
    z[0:64, 0:64] = wqr.T
    z[64:128, 64:128] = wqr.T
    d["wqr_bd"] = z.copy()
    z[:] = 0
    z[0:64, 0:64] = wkr.T
    z[64:128, 64:128] = wkr.T
    d["wkr_bd"] = z.copy()
    d["wqr_raw"] = np.ascontiguousarray(wqr)
    d["wkr_raw"] = np.ascontiguousarray(wkr)
    d["woT"] = np.ascontiguousarray(inp["L0_attn_W_O"].T)
    z = np.zeros((128, 128), np.float32)
    z[0:64, 0:64] = inp["L0_attn_W_O"].T
    z[64:128, 64:128] = inp["L0_attn_W_O"].T
    d["wo_bd"] = z.copy()
    qnw = inp["L0_attn_q_norm_w"]
    knw = inp["L0_attn_k_norm_w"]
    d["qnw_row"] = qnw[None, :].copy()
    d["knw_row"] = knw[None, :].copy()
    z = np.zeros((2, 128), np.float32)
    z[0, 0:64] = qnw
    z[1, 64:128] = qnw
    d["qbcast2"] = z.copy()
    z[:] = 0
    z[0, 0:64] = knw
    z[1, 64:128] = knw
    d["kbcast2"] = z.copy()
    d["gwT"] = np.ascontiguousarray(
        inp["L0_ffn_gate_weight"].T.reshape(8, 128, 8).transpose(1, 0, 2))
    d["gbias"] = np.ascontiguousarray(inp["L0_ffn_gate_bias"][:, None])
    perm = np.empty(2048, np.int64)
    for c in range(8):
        ii = np.arange(128) + 128 * c
        perm[256 * c:256 * c + 128] = 2 * ii
        perm[256 * c + 128:256 * c + 256] = 2 * ii + 1
    d["w1pT"] = np.ascontiguousarray(
        inp["L0_ffn_mlp1_weight"].transpose(0, 2, 1)[:, :, perm])
    b1p = inp["L0_ffn_mlp1_bias"][:, perm].reshape(8, 8, 2, 128)
    d["b1g_cols"] = np.ascontiguousarray(
        b1p[:, :, 0, :].transpose(2, 0, 1).reshape(128, 64))
    d["b1l_cols"] = np.ascontiguousarray(
        b1p[:, :, 1, :].transpose(2, 0, 1).reshape(128, 64))
    d["w2T"] = np.ascontiguousarray(inp["L0_ffn_mlp2_weight"].transpose(0, 2, 1))
    d["b2"] = np.ascontiguousarray(inp["L0_ffn_mlp2_bias"])
    d["nw_cols"] = np.ascontiguousarray(inp["L0_ffn_norm_weight"].reshape(8, 128).T)
    d["ident"] = np.eye(128, dtype=bf16)
    d["ident32"] = np.eye(128, dtype=np.float32)
    d["masks"] = (np.arange(128)[:, None]
                  <= (np.arange(896)[None, :] - 384)).astype(bf16)
    d["iota8"] = np.tile(np.arange(8, dtype=np.float32), (128, 1))
    d["iotaB"] = d["iota8"] + BIG
    d["ones1x128"] = np.ones((1, 128), np.float32)
    d["ones64"] = np.ones((64, 1), np.float32)
    z = np.zeros((128, 2), np.float32)
    z[0:64, 0] = 1.0
    z[64:128, 1] = 1.0
    d["ones2"] = z
    sel = np.zeros((8, 8, 128), bf16)
    for e in range(8):
        sel[e, e, :] = bf16(1.0)
    d["selE"] = sel
    return d


def build_program(debug_dumps=(), stop_after=None):
    nc = bass.Bass("TRN2", target_bir_lowering=False,
                   dynamic_dma_scratch_size=4096)

    def din(name, shape, dtype):
        return nc.dram_tensor(name, shape, dtype, kind="ExternalInput").ap()

    x_own = din("x_own", [B, TOKC, 1024], F32)
    wdqT = din("wdqT", [128, 512], F32)
    wuqT = din("wuqT", [128, 4, 64], F32)
    wdkvT = din("wdkvT", [128, 1024], F32)
    wukvT = din("wukvT", [128, 8, 128], F32)
    wqr_bd = din("wqr_bd", [128, 128], F32)
    wkr_bd = din("wkr_bd", [128, 128], F32)
    wqr_raw = din("wqr_raw", [64, 64], F32)
    wkr_raw = din("wkr_raw", [64, 64], F32)
    woT = din("woT", [64, 64], F32)
    wo_bd = din("wo_bd", [128, 128], F32)
    qnw_row = din("qnw_row", [1, 64], F32)
    knw_row = din("knw_row", [1, 64], F32)
    qbcast2 = din("qbcast2", [2, 128], F32)
    kbcast2 = din("kbcast2", [2, 128], F32)
    gwT_in = din("gwT", [128, 8, 8], BF16)
    gbias = din("gbias", [8, 1], BF16)
    w1pT = din("w1pT", [8, 1024, 2048], BF16)
    b1g_cols = din("b1g_cols", [128, 64], BF16)
    b1l_cols = din("b1l_cols", [128, 64], BF16)
    w2T = din("w2T", [8, 1024, 1024], BF16)
    b2_in = din("b2", [8, 1024], BF16)
    nw_cols = din("nw_cols", [128, 8], F32)
    ident_in = din("ident", [128, 128], BF16)
    ident32_in = din("ident32", [128, 128], F32)
    masks_in = din("masks", [128, 896], BF16)
    iota8_in = din("iota8", [128, 8], F32)
    iotaB_in = din("iotaB", [128, 8], F32)
    ones1x128 = din("ones1x128", [1, 128], F32)
    ones64 = din("ones64", [64, 1], F32)
    ones2_in = din("ones2", [128, 2], F32)
    selE_in = din("selE", [8, 8, 128], BF16)

    out_own = nc.dram_tensor("out_own", [B, TOKC, 1024], F32,
                             kind="ExternalOutput").ap()
    ag1_in = nc.dram_tensor("ag1_in", [8 * 128, 2048], BF16).ap()
    ag1_out = nc.dram_tensor("ag1_out", [NCORE * 8 * 128, 2048], BF16,
                             addr_space="Shared").ap()
    ag2_in = nc.dram_tensor("ag2_in", [B * TSEQ, 128], BF16).ap()
    ahat_d = nc.dram_tensor("ahat_d", [4, 8, 8, 128, 512], BF16).ap()
    ag2_out = nc.dram_tensor("ag2_out", [NCORE * B * TSEQ, 128], BF16,
                             addr_space="Shared").ap()

    dbg_drams = {}
    for (name, shape, dtype) in debug_dumps:
        dbg_drams[name] = nc.dram_tensor("dbg_" + name, shape, dtype,
                                         kind="ExternalOutput").ap()

    P = Planner()
    st = ExitStack()

    def sb(name, shape, dtype):
        return st.enter_context(nc.sbuf_tensor(name, shape, dtype))

    xc = sb("xc", [128, NTILE, 1024], BF16)
    nfT = sb("nfT", [128, 8, 2048], BF16)
    kvq = sb("kvq", [128, 49408], BF16)
    xr = sb("xr", [128, 2, 1024], F32)
    w_dq = sb("w_dq", [128, 512], BF16)
    w_uq = sb("w_uq", [128, 4, 64], BF16)
    w_dkv = sb("w_dkv", [128, 1024], BF16)
    w_ukv = sb("w_ukv", [128, 8, 128], BF16)
    w_qrbd = sb("w_qrbd", [128, 128], BF16)
    w_krbd = sb("w_krbd", [128, 128], BF16)
    w_wobd = sb("w_wobd", [128, 128], BF16)
    w_qb2 = sb("w_qb2", [2, 128], BF16)
    w_kb2 = sb("w_kb2", [2, 128], BF16)
    w_ones2 = sb("w_ones2", [128, 2], BF16)
    w_gw = sb("w_gw", [128, 8, 8], BF16)
    w_gb = sb("w_gb", [8, 1], F32)
    w_b2 = sb("w_b2", [8, 1024], BF16)
    w_sel = sb("w_sel", [8, 8, 128], BF16)
    w_b1g = sb("w_b1g", [128, 64], F32)
    w_b1l = sb("w_b1l", [128, 64], F32)
    w_nw = sb("w_nw", [128, 8], F32)
    c_ident = sb("c_ident", [128, 128], BF16)
    c_ident32 = sb("c_ident32", [128, 128], F32)
    c_masks = sb("c_masks", [128, 896], BF16)
    c_iota = sb("c_iota", [128, 8], F32)
    c_iotaB = sb("c_iotaB", [128, 8], F32)
    c_ones1x128 = sb("c_ones1x128", [1, 128], F32)
    c_ones64 = sb("c_ones64", [64, 1], F32)
    mwork = sb("mwork", [64, 64], F32)
    mred = sb("mred", [64, 1], F32)
    mq_raw = sb("mq_raw", [1, 64], F32)
    mk_raw = sb("mk_raw", [1, 64], F32)
    msc = sb("msc", [1, 8], F32)
    negM = sb("negM", [128, 1], F32)
    sqs = sb("sqs", [128, 2, 1024], BF16)
    stat = sb("stat", [128, 2, 12], F32)
    epsA = sb("epsA", [128, 1], F32)
    eps2 = sb("eps2", [2, 1], F32)
    xhat = sb("xhat", [128, 2, 1024], BF16)
    lg_tm = sb("lg_tm", [128, NTILE, 8], F32)
    t2 = sb("t2", [128, 48], F32)
    comb_tm = sb("comb_tm", [128, 8], BF16)
    combT = sb("combT", [8, 2048], BF16)
    cqs = sb("cqs", [128, 2, 2, 512], BF16)
    qc_sb = sb("qc_sb", [128, 512], BF16)
    sq_sb = sb("sq_sb", [128, 512], BF16)
    rln = sb("rln", [2, 512], F32)
    rstd2r = sb("rstd2r", [2, 512], BF16)
    qcn_sb = sb("qcn_sb", [128, 512], BF16)
    qr_sb = sb("qr_sb", [128, 512], BF16)
    kvsb = sb("kvsb", [128, 2, 512], BF16)
    kcp = sb("kcp", [128, 512], BF16)
    vp = sb("vp", [128, 512], BF16)
    recip = sb("recip", [128, 4], F32)
    tp = sqs.ap().rearrange("p a n -> p a n")
    _xh_flat = xhat.ap().rearrange("p a n -> p (a n)")
    ctxsb = _xh_flat[0:65, 0:512]
    attst = _xh_flat[:, 1024:1536].rearrange("p (s k d) -> p s k d", s=2, k=4)
    sfin = sb("sfin", [128, 1024], BF16)
    xatt = sb("xatt", [128, 1024], BF16)
    ffin = sb("ffin", [128, 1024], F32)

    ps4 = st.enter_context(nc.psum_tensor("ps4", [128, 3072], F32))
    pc = st.enter_context(nc.psum_tensor("pc", [128, 512], F32))
    pbt = st.enter_context(nc.psum_tensor("pbt", [128, 1024], BF16))
    pb4 = ps4.ap()[:, 2048:2560]
    pb5 = ps4.ap()[:, 2560:3072]

    kv = kvq.ap()
    xnT = kv[:, 0:16384].rearrange("p (c t) -> p c t", c=8)
    xh2 = kv[:, 0:8192]
    lgT = kv[0:8, 16384:18432]
    combsb = kv[:, 38912:43008].rearrange("p (h t) -> p h t", h=8)
    besb = kv[:, 43008:43520]
    gsb = kv[:, 43520:44032]
    silu_sb = kv[:, 44032:44544]
    lsb = kv[:, 44544:45056]
    act_tmp = kv[:, 45056:45568]
    K2 = [kv[:, 8192:16384], kv[:, 16384:24576]]
    Q2 = [kv[:, 24576:32768], kv[:, 32768:40960]]
    V2 = [kv[:, 40960:45184].rearrange("p (a c) -> p a c", c=66),
          kv[:, 45184:49408].rearrange("p (a c) -> p a c", c=66)]
    ahat = kv[:, 0:32768].rearrange("p (e i t) -> p e i t", e=8, i=8)
    w1f = [kv[:, 1024 * i:1024 * (i + 1)].rearrange("p (c t) -> p c t", c=8)
           for i in range(4)]
    f_act = kv[:, 4096:4608]
    f_ah = [kv[:, 4608:5120], kv[:, 5120:5632]]
    f_be = kv[:, 6144:6656]
    f_g = kv[:, 6656:7168]
    f_si = kv[:, 7168:7680]
    f_l = kv[:, 7680:8192]
    _w1r2 = kv[:, 32768:36864].rearrange("p (r c t) -> p r c t", r=2, c=8)
    _w2r2 = kv[:, 36864:38912].rearrange("p (r c t) -> p r c t", r=2, c=8)
    w1slots = [_w1r2[:, 0], _w1r2[:, 1],
               kv[:, 46592:48640].rearrange("p (c t) -> p c t", c=8)]
    w2slots = [_w2r2[:, 0], _w2r2[:, 1],
               kv[:, 45568:46592].rearrange("p (c t) -> p c t", c=8),
               kv[:, 32768:33792].rearrange("p (c t) -> p c t", c=8),
               kv[:, 33792:34816].rearrange("p (c t) -> p c t", c=8),
               kv[:, 34816:35840].rearrange("p (c t) -> p c t", c=8)]
    Q0, Q1, Q2_, Q3 = (ps4.ap()[:, 0:512], ps4.ap()[:, 512:1024],
                       ps4.ap()[:, 1024:1536], ps4.ap()[:, 1536:2048])

    x_toks = []
    x_free = []

    def p0():
        plain = [(w_gw, gwT_in), (w_b2, b2_in), (w_sel, selE_in),
                 (w_nw, nw_cols), (c_ident, ident_in),
                 (c_ident32, ident32_in), (c_masks, masks_in),
                 (c_iota, iota8_in), (c_iotaB, iotaB_in),
                 (c_ones1x128, ones1x128), (c_ones64, ones64),
                 (mq_raw, qnw_row), (mk_raw, knw_row), (mwork, wqr_raw)]
        tw = None
        for dst, src in plain:
            tw = P.op("sync", lambda sy, d=dst, s=src:
                      sy.dma_start(out=d[...], in_=s[...]), n_dma=1,
                      chan="wload")
        cast = [(w_dq, wdqT), (w_uq, wuqT), (w_dkv, wdkvT), (w_ukv, wukvT),
                (w_qrbd, wqr_bd), (w_krbd, wkr_bd), (w_wobd, wo_bd),
                (w_qb2, qbcast2), (w_kb2, kbcast2), (w_ones2, ones2_in),
                (w_gb, gbias), (w_b1g, b1g_cols), (w_b1l, b1l_cols)]
        tc = None
        for dst, src in cast:
            tc = P.op("gpsimd", lambda g, d=dst, s=src:
                      g.dma_start(out=d[...], in_=s[...]), n_dma=1,
                      chan="gload")
        for t in range(2):
            b, ti = t // 8, t % 8
            x_toks.append(P.op(
                "sync",
                lambda sy, b=b, ti=ti, t=t: sy.dma_start(
                    out=xr[:, t % 2, :],
                    in_=x_own[b].rearrange("(n p) d -> n p d", p=128)[ti]),
                n_dma=1, chan="x%d" % (t % 2)))
        te = P.op("vector", lambda v: v.memset(epsA[:, :], LN_EPS))
        te2 = P.op("vector", lambda v: v.memset(eps2[:, :], RMS_EPS))
        tv1 = P.op("vector", lambda v: v.memset(V2[0][:, :, 64:65], 1.0))
        tv2 = P.op("vector", lambda v: v.memset(V2[1][:, :, 64:65], 1.0))
        ta = P.op("scalar", lambda s: s.activation(mwork[:, :], mwork[:, :],
                                                   AF.Square), waits=[tw, tc])
        tvr = P.op("vector", lambda v: v.reduce_sum(mred[:, 0:1], mwork[:, :],
                                                    axis=AX.X), waits=[ta])
        tt = P.op("tensor", lambda t: t.matmul(pc[0:1, 0:1], c_ones64[:, 0:1],
                                               mred[:, 0:1], start=True,
                                               stop=True), waits=[tvr])
        tv = P.op("vector", lambda v: v.tensor_copy(msc[0:1, 0:1], pc[0:1, 0:1]),
                  waits=[tt])
        tg = P.op("gpsimd", lambda g: g.dma_start(out=mwork[:, :], in_=wkr_raw),
                  waits=[tvr], n_dma=1, chan="gload")
        ta = P.op("scalar", lambda s: s.activation(mwork[:, :], mwork[:, :],
                                                   AF.Square), waits=[tg])
        tv2b = P.op("vector", lambda v: v.reduce_sum(mred[:, 0:1], mwork[:, :],
                                                     axis=AX.X), waits=[ta, tv])
        tt = P.op("tensor", lambda t: t.matmul(pc[0:1, 1:2], c_ones64[:, 0:1],
                                               mred[:, 0:1], start=True,
                                               stop=True), waits=[tv2b])
        seq = [
            lambda v: v.tensor_copy(msc[0:1, 1:2], pc[0:1, 1:2]),
            lambda v: v.tensor_mul(mwork[0:1, 0:64], mq_raw[0:1, :],
                                   mq_raw[0:1, :]),
            lambda v: v.reduce_max(msc[0:1, 2:3], mwork[0:1, 0:64], axis=AX.X),
            lambda v: v.tensor_mul(mwork[0:1, 0:64], mk_raw[0:1, :],
                                   mk_raw[0:1, :]),
            lambda v: v.reduce_max(msc[0:1, 3:4], mwork[0:1, 0:64], axis=AX.X),
            lambda v: v.tensor_scalar(msc[0:1, 4:5], msc[0:1, 0:1], 1.0, 64.0,
                                      ALU.add, ALU.mult),
            lambda v: v.tensor_mul(msc[0:1, 4:5], msc[0:1, 4:5], msc[0:1, 2:3]),
            lambda v: v.tensor_scalar_mul(msc[0:1, 5:6], msc[0:1, 1:2], 1024.0),
            lambda v: v.scalar_tensor_tensor(msc[0:1, 5:6], msc[0:1, 3:4], 64.0,
                                             msc[0:1, 5:6], ALU.mult, ALU.add),
            lambda v: v.tensor_mul(msc[0:1, 6:7], msc[0:1, 4:5], msc[0:1, 5:6]),
        ]
        tok = tt
        for fn in seq:
            tok = P.op("vector", fn, waits=[tok])
        tok = P.op("scalar", lambda s: s.activation(msc[0:1, 7:8], msc[0:1, 6:7],
                                                    AF.Ln), waits=[tok])
        tok = P.op("scalar", lambda s: s.activation(msc[0:1, 7:8], msc[0:1, 7:8],
                                                    AF.Exp, scale=0.5),
                   waits=[tok])
        tok = P.op("vector", lambda v: v.tensor_scalar(
            msc[0:1, 7:8], msc[0:1, 7:8], -0.125, -50.0,
            ALU.mult, ALU.max), waits=[tok])
        tok = P.op("tensor", lambda t: t.matmul(
            pc[:, 2:3], c_ones1x128[0:1, :], msc[0:1, 7:8],
            start=True, stop=True), waits=[tok])
        tok = P.op("vector", lambda v: v.tensor_copy(negM[:, 0:1], pc[:, 2:3]),
                   waits=[tok])
        return tok, te, te2, (tv1, tv2)

    negM_tok, eps_tok, eps2_tok, vones_toks = p0()

    xnT_toks = [None] * NTILE
    nfT_toks = [None] * NTILE
    xc_toks = [None] * NTILE

    def p1a():
        prev_nf = None
        prev_tf2 = None
        prev_te1 = None
        for t in range(NTILE):
            r = t % 2
            xt = xr.ap()[:, t % 2, :]
            sq = sqs.ap()[:, r, :]
            s_ = stat.ap()[:, r, :]
            xh = xhat.ap()[:, r, :]
            tv = P.op("vector", lambda v, xt=xt, s_=s_:
                      v.reduce_sum(s_[:, 0:1], xt, axis=AX.X),
                      waits=[x_toks[t], negM_tok])
            tv = P.op("vector", lambda v, xt=xt, sq=sq, s_=s_:
                      v.scalar_tensor_tensor(sq, xt, 1.0, xt, ALU.mult,
                                             ALU.mult,
                                             accum_out=s_[:, 1:2]),
                      waits=[tv])
            tv = P.op("vector", lambda v, s_=s_:
                      v.tensor_scalar_mul(s_[:, 2:3], s_[:, 0:1], 1.0 / 1024.0),
                      waits=[tv])
            tv = P.op("vector", lambda v, s_=s_:
                      v.tensor_mul(s_[:, 3:4], s_[:, 2:3], s_[:, 2:3]),
                      waits=[tv])
            tv = P.op("vector", lambda v, s_=s_:
                      v.scalar_tensor_tensor(s_[:, 4:5], s_[:, 1:2],
                                             1.0 / 1024.0, s_[:, 3:4],
                                             ALU.mult, ALU.subtract),
                      waits=[tv])
            ta = P.op("scalar", lambda s, s_=s_:
                      s.activation(s_[:, 8:9], s_[:, 4:5], AF.Ln,
                                   bias=epsA[:, 0:1]), waits=[tv, eps_tok])
            ta = P.op("scalar", lambda s, s_=s_:
                      s.activation(s_[:, 5:6], s_[:, 8:9], AF.Exp, scale=-0.5),
                      waits=[ta])
            tn = P.op("vector", lambda v, s_=s_: v.tensor_single_scalar(
                s_[:, 10:11], s_[:, 4:5], epsA[:, 0:1], ALU.add), waits=[ta])
            tn = P.op("vector", lambda v, s_=s_: v.tensor_mul(
                s_[:, 11:12], s_[:, 5:6], s_[:, 5:6]), waits=[tn])
            tn = P.op("vector", lambda v, s_=s_: v.tensor_mul(
                s_[:, 11:12], s_[:, 11:12], s_[:, 10:11]), waits=[tn])
            tn = P.op("vector", lambda v, s_=s_: v.tensor_scalar(
                s_[:, 11:12], s_[:, 11:12], -0.5, 1.5, ALU.mult, ALU.add),
                waits=[tn])
            tn = P.op("vector", lambda v, s_=s_: v.tensor_mul(
                s_[:, 5:6], s_[:, 5:6], s_[:, 11:12]), waits=[tn])
            txc = P.op("vector", lambda v, xt=xt, s_=s_, t=t:
                       v.tensor_scalar(xc[:, t, :], xt, s_[:, 2:3], s_[:, 5:6],
                                       ALU.subtract, ALU.mult), waits=[tn])
            xc_toks[t] = txc
            x_free.append(txc)
            tv = P.op("vector", lambda v, sq=sq, s_=s_, t=t:
                      v.scalar_tensor_tensor(sq, xc[:, t, :], 1.0,
                                             xc[:, t, :], ALU.mult, ALU.mult,
                                             accum_out=s_[:, 6:7]),
                      waits=[txc])
            ta = P.op("scalar", lambda s, s_=s_:
                      s.activation(s_[:, 9:10], s_[:, 6:7], AF.Ln,
                                   bias=epsA[:, 0:1], scale=1.0 / 1024.0),
                      waits=[tv])
            ta = P.op("scalar", lambda s, s_=s_:
                      s.activation(s_[:, 7:8], s_[:, 9:10], AF.Exp, scale=-0.5),
                      waits=[ta])
            tn = P.op("vector", lambda v, s_=s_: v.tensor_scalar(
                s_[:, 10:11], s_[:, 6:7], 1.0 / 1024.0, epsA[:, 0:1],
                ALU.mult, ALU.add), waits=[ta])
            tn = P.op("vector", lambda v, s_=s_: v.tensor_mul(
                s_[:, 11:12], s_[:, 7:8], s_[:, 7:8]), waits=[tn])
            tn = P.op("vector", lambda v, s_=s_: v.tensor_mul(
                s_[:, 11:12], s_[:, 11:12], s_[:, 10:11]), waits=[tn])
            tn = P.op("vector", lambda v, s_=s_: v.tensor_scalar(
                s_[:, 11:12], s_[:, 11:12], -0.5, 1.5, ALU.mult, ALU.add),
                waits=[tn])
            tn = P.op("vector", lambda v, s_=s_: v.tensor_mul(
                s_[:, 7:8], s_[:, 7:8], s_[:, 11:12]), waits=[tn])
            txh = P.op("vector", lambda v, s_=s_, t=t:
                       v.tensor_scalar_mul(ffin[:, :], xc[:, t, :],
                                           s_[:, 7:8]),
                       waits=[tn, prev_tf2])

            def mktrs(src):
                def f(tt):
                    last = None
                    for c in range(8):
                        last = tt.transpose(pbt[:, 128 * c:128 * (c + 1)],
                                            src[:, 128 * c:128 * (c + 1)],
                                            c_ident[:, :])
                    return last
                return f
            tf1 = P.op("tensor", mktrs(xc.ap()[:, t, :]),
                       waits=[txc, prev_te1])
            te1 = P.op("vector", lambda v, t=t: v.tensor_copy(
                xnT[:, :, 128 * t:128 * (t + 1)],
                pbt.ap().rearrange("p (c n) -> p c n", c=8)), waits=[tf1])
            xnT_toks[t] = te1
            prev_te1 = te1

            def f32trs(tt):
                last = None
                for c in range(8):
                    last = tt.transpose(ps4[:, 128 * c:128 * (c + 1)],
                                        ffin[:, 128 * c:128 * (c + 1)],
                                        c_ident32[:, :])
                return last
            tf2 = P.op("tensor", f32trs, waits=[txh, prev_nf])

            def nfev(s, t=t):
                last = None
                for c in range(8):
                    last = s.activation(nfT[:, c, 128 * t:128 * (t + 1)],
                                        ps4[:, 128 * c:128 * (c + 1)],
                                        AF.Copy, scale=w_nw[:, c:c + 1])
                return last
            te2 = P.op("scalar", nfev, waits=[tf2])
            nfT_toks[t] = te2
            prev_nf = te2
            prev_tf2 = tf2
            tn = t + 2
            if tn < NTILE:
                bn, tin = tn // 8, tn % 8
                x_toks.append(P.op(
                    "sync",
                    lambda sy, b=bn, ti=tin, t=tn: sy.dma_start(
                        out=xr[:, t % 2, :],
                        in_=x_own[b].rearrange("(n p) d -> n p d", p=128)[ti]),
                    waits=[txc], n_dma=1, chan="x%d" % (tn % 2)))

    p1a()

    def ex1():
        tok = None
        for c in range(8):
            tok = P.op("sync", lambda sy, c=c: sy.dma_start(
                out=ag1_in.rearrange("(c p) t -> c p t", p=128)[c],
                in_=xnT[:, c, :]), waits=xnT_toks, n_dma=1, chan="ag1send")
        return P.collective(lambda g: g.collective_compute(
            "AllGather", ALU.bypass, replica_groups=[list(range(NCORE))],
            ins=[ag1_in[:, :]], outs=[ag1_out[:, :]]), waits=[tok],
            chan="cc1")

    cc1_tok = ex1()

    def p1b():
        lgT_toks = []
        pxfree = negM_tok
        for tb in range(4):
            def gmm(t, tb=tb):
                last = None
                for c in range(8):
                    last = t.matmul(pc[0:8, :], w_gw[:, c, :],
                                    nfT[:, c, 512 * tb:512 * (tb + 1)],
                                    start=(c == 0), stop=(c == 7))
                return last
            tmm = P.op("tensor", gmm,
                       waits=[nfT_toks[4 * tb + i] for i in range(4)] + [pxfree])
            tev = P.op("vector", lambda v, tb=tb: v.tensor_single_scalar(
                lgT[:, 512 * tb:512 * (tb + 1)], pc[0:8, :], w_gb[:, 0:1],
                ALU.add), waits=[tmm])
            pxfree = tev
            lgT_toks.append(tev)
        lg_toks = []
        prev_ev = xnT_toks[-1]
        for tb in range(4):
            def ltr(t, tb=tb):
                last = None
                for k in range(4):
                    last = t.transpose(
                        pbt[:, 8 * k:8 * (k + 1)],
                        lgT[0:8, 512 * tb + 128 * k:512 * tb + 128 * (k + 1)],
                        c_ident[0:8, 0:8])
                return last
            tf = P.op("tensor", ltr, waits=[lgT_toks[tb], prev_ev])
            prev_ev = P.op("vector", lambda v, tb=tb: v.tensor_copy(
                lg_tm[:, 4 * tb:4 * (tb + 1), :],
                pbt.ap().rearrange("p (k e) -> p k e", k=128)[:, 0:4, :]),
                waits=[tf])
            lg_toks.append(prev_ev)
        t2a = t2.ap()
        m1, i1, m2_, i2 = (t2a[:, 0:1], t2a[:, 1:2], t2a[:, 2:3], t2a[:, 3:4])
        w2t, w1t = t2a[:, 4:5], t2a[:, 5:6]
        tt1 = t2a[:, 8:16]
        mi = t2a[:, 16:24]
        sel1 = t2a[:, 24:32]
        l2 = t2a[:, 32:40]
        sel2 = t2a[:, 40:48]
        prev_ctr = prev_ev
        for t in range(NTILE):
            lg = lg_tm.ap()[:, t, :]
            tok = P.op("vector", lambda v, lg=lg:
                       v.reduce_max(m1, lg, axis=AX.X),
                       waits=[lg_toks[t // 4], prev_ctr])
            chain = [
                lambda v, lg=lg: v.tensor_scalar(tt1, lg, m1, None, ALU.is_ge),
                lambda v: v.scalar_tensor_tensor(mi, tt1, -BIG, c_iotaB[:, :],
                                                 ALU.mult, ALU.add),
                lambda v: v.tensor_reduce(i1, mi, axis=AX.X, op=ALU.min),
                lambda v: v.tensor_scalar(sel1, c_iota[:, :], i1, None,
                                          ALU.is_equal),
                lambda v, lg=lg: v.scalar_tensor_tensor(l2, sel1, -BIG, lg,
                                                        ALU.mult, ALU.add),
                lambda v: v.reduce_max(m2_, l2, axis=AX.X),
                lambda v: v.tensor_sub(w2t, m2_, m1),
            ]
            for fn in chain:
                tok = P.op("vector", fn, waits=[tok])
            tok = P.op("scalar", lambda s: s.activation(w2t, w2t, AF.Sigmoid),
                       waits=[tok])
            chain2 = [
                lambda v: v.tensor_scalar(sel2, l2, m2_, None, ALU.is_ge),
                lambda v: v.scalar_tensor_tensor(mi, sel2, -BIG, c_iotaB[:, :],
                                                 ALU.mult, ALU.add),
                lambda v: v.tensor_reduce(i2, mi, axis=AX.X, op=ALU.min),
                lambda v: v.tensor_scalar(sel2, c_iota[:, :], i2, None,
                                          ALU.is_equal),
                lambda v: v.tensor_scalar(w1t, w2t, -1.0, 1.0, ALU.mult,
                                          ALU.add),
                lambda v: v.tensor_scalar_mul(comb_tm[:, :], sel1, w1t),
                lambda v: v.scalar_tensor_tensor(comb_tm[:, :], sel2, w2t,
                                                 comb_tm[:, :], ALU.mult,
                                                 ALU.add),
            ]
            for fn in chain2:
                tok = P.op("vector", fn, waits=[tok])
            tf = P.op("tensor", lambda t_: t_.transpose(
                pbt[0:8, 0:128], comb_tm[:, :], c_ident[:, :]), waits=[tok])
            prev_ctr = P.op("vector", lambda v, t=t: v.tensor_copy(
                combT[0:8, 128 * t:128 * (t + 1)], pbt[0:8, 0:128]),
                waits=[tf])
        return prev_ctr

    combT_tok = p1b()

    pid_sync = nc.partition_id(engines=[ET.SP])
    att_dma_toks = []
    xh2_readers_prev = []
    pbt_owner = combT_tok
    pcF = [combT_tok]
    b4F = [negM_tok]
    b5F = [negM_tok]
    flash_last_pe = None

    def p2_batch(b, flash_done_prev):
        nonlocal pbt_owner, flash_last_pe
        ag1v = ag1_out.rearrange("(r q p) t -> q p r t", r=NCORE, q=8)
        xh2v = xh2.rearrange("p (r n) -> p r n", r=NCORE)
        wv = [cc1_tok] + xh2_readers_prev + ([flash_done_prev] if flash_done_prev else [])
        xh2_ld = P.op("sync", lambda sy: sy.dma_start(
            out=xh2v,
            in_=ag1v[pid_sync, :, :, 1024 * b:1024 * (b + 1)]),
            waits=wv, n_dma=1, chan="xh2")
        xh2_rd = []
        q_toks = [None] * 16
        k_toks = [None] * 16
        prevq = [combT_tok if b == 0 else flash_done_prev]
        pend = []

        def flush_p2():
            while pend:
                pend.pop(0)()

        def waves(nch, wmat, accA_bank, accB_bank, accW, accA_waits,
                  accB_waits, ts_):
            ev_toks = [None, None]
            acc_last = [None]
            for c in range(nch):
                pg = c % 2
                qa = ps4.ap()[:, 512 * pg:512 * (pg + 1)]
                qb_ = ps4.ap()[:, 1024 + 512 * pg:1024 + 512 * (pg + 1)]
                wv = [xh2_ld]
                if c >= 2:
                    wv.append(ev_toks[pg])
                else:
                    wv.extend(prevq)
                P.op("tensor", lambda t, c=c, qa=qa, ts_=ts_, wmat=wmat:
                     t.matmul(qa, wmat[0:64, 128 * c:128 * (c + 1)],
                              xh2[0:64, ts_], start=True, stop=True),
                     waits=wv, token=False)
                tB = P.op("tensor", lambda t, c=c, qb_=qb_, ts_=ts_, wmat=wmat:
                          t.matmul(qb_, wmat[64:128, 128 * c:128 * (c + 1)],
                                   xh2[64:128, ts_], start=True, stop=True))
                xh2_rd.append(tB)
                if c == 2:
                    flush_p2()
                tevA = P.op("scalar", lambda sc, qa=qa, pg=pg:
                            sc.activation(cqs[:, 0, pg, :], qa, AF.Copy),
                            waits=[tB])
                tevB = P.op("vector", lambda v, qb_=qb_, pg=pg:
                            v.tensor_copy(cqs[:, 1, pg, :], qb_),
                            waits=[tB, tevA])
                ev_toks[pg] = tevB
                wA = [tevB] + (accA_waits if c == 0 else [])
                P.op("tensor", lambda t, c=c, pg=pg, accA_bank=accA_bank,
                     accW=accW: t.matmul(accA_bank, accW[:, c, :],
                                         cqs[:, 0, pg, :],
                                         start=(c == 0), stop=(c == nch - 1)),
                     waits=wA, token=False)
                wB = accB_waits if c == 0 else []
                acc_last[0] = P.op(
                    "tensor", lambda t, c=c, pg=pg, accB_bank=accB_bank,
                    accW=accW: t.matmul(accB_bank, accW[:, c, :],
                                        cqs[:, 1, pg, :],
                                        start=(c == 0), stop=(c == nch - 1)))
            return ev_toks, acc_last[0]

        for tb in range(16):
            ts_ = slice(512 * tb, 512 * (tb + 1))
            ev_toks, tacc = waves(4, w_dq, pb4[0:64, :], pb5[0:64, :], w_uq,
                                  [b4F[0]], [b5F[0]], ts_)
            tqc = P.op("scalar", lambda sc: sc.activation(
                qc_sb[0:64, :], pb4[0:64, :], AF.Copy), waits=[tacc])
            tqcB = P.op("vector", lambda v: v.tensor_copy(
                qc_sb[64:128, :], pb5[0:64, :]), waits=[tacc, tqc])

            def bundle_q(tb=tb, ts_=ts_, tqc=tqc, tqcB=tqcB):
                tsq = P.op("scalar", lambda s: s.activation(
                    sq_sb[:, :], qc_sb[:, :], AF.Square), waits=[tqcB])
                tss = P.op("tensor", lambda t: t.matmul(
                    pc[0:2, :], w_ones2[:, :], sq_sb[:, :],
                    start=True, stop=True), waits=[tsq, pcF[0], eps2_tok])
                tln = P.op("scalar", lambda s: s.activation(
                    rln[:, :], pc[0:2, :], AF.Ln, bias=eps2[:, 0:1],
                    scale=1.0 / 64.0), waits=[tss])
                trs = P.op("scalar", lambda s: s.activation(
                    rstd2r[:, :], rln[:, :], AF.Exp, scale=-0.5), waits=[tln])
                tb2 = P.op("tensor", lambda t: t.matmul(
                    pc[:, :], w_qb2[:, :], rstd2r[:, :], start=True,
                    stop=True), waits=[trs])
                tqcn = P.op("vector", lambda v: v.tensor_mul(
                    qcn_sb[:, :], pc[:, :], qc_sb[:, :]), waits=[tb2])
                tqr = P.op("tensor", lambda t: t.matmul(
                    pc[:, :], w_qrbd[:, :], qcn_sb[:, :], start=True,
                    stop=True), waits=[tqcn])
                tqrev = P.op("vector", lambda v: v.tensor_copy(
                    qr_sb[:, :], pc[:, :]), waits=[tqr])
                pcF[0] = tqrev
                P.op("vector", lambda v, ts_=ts_: v.tensor_copy(
                    Q2[0][0:64, ts_], qcn_sb[0:64, :]), waits=[tqcn],
                    token=False)
                P.op("vector", lambda v, ts_=ts_: v.tensor_copy(
                    Q2[0][64:128, ts_], qr_sb[0:64, :]), waits=[tqrev],
                    token=False)
                P.op("vector", lambda v, ts_=ts_: v.tensor_copy(
                    Q2[1][0:64, ts_], qcn_sb[64:128, :]), waits=[tqcn],
                    token=False)
                tq4 = P.op("vector", lambda v, ts_=ts_: v.tensor_copy(
                    Q2[1][64:128, ts_], qr_sb[64:128, :]), waits=[tqrev])
                q_toks[tb] = tq4
            pend.append(bundle_q)
            ev_toks, tacc = waves(8, w_dkv, pb5, pb4, w_ukv,
                                  [tqcB], [tqc], ts_)
            prevq = list(ev_toks)
            tkv0 = P.op("scalar", lambda sc: sc.activation(
                kvsb[:, 0, :], pb5[:, :], AF.Copy), waits=[tacc])
            tkv1 = P.op("vector", lambda v: v.tensor_copy(
                kvsb[:, 1, :], pb4[:, :]), waits=[tacc, tkv0])
            b5F[0] = tkv0
            b4F[0] = tkv1
            tk1 = P.op("vector", lambda v: v.tensor_copy(
                kcp[0:64, :], kvsb[0:64, 0, :]), waits=[tkv0])
            tk2 = P.op("vector", lambda v: v.tensor_copy(
                kcp[64:128, :], kvsb[0:64, 1, :]), waits=[tkv1, tk1])
            tv1 = P.op("vector", lambda v: v.tensor_copy(
                vp[0:64, :], kvsb[64:128, 0, :]), waits=[tk2])
            tv2 = P.op("vector", lambda v: v.tensor_copy(
                vp[64:128, :], kvsb[64:128, 1, :]), waits=[tv1])

            def bundle_k(tb=tb, ts_=ts_, tk2=tk2, tv2=tv2):
                tsq = P.op("scalar", lambda s: s.activation(
                    sq_sb[:, :], kcp[:, :], AF.Square), waits=[tk2])
                tss = P.op("tensor", lambda t: t.matmul(
                    pc[0:2, :], w_ones2[:, :], sq_sb[:, :],
                    start=True, stop=True), waits=[tsq, pcF[0]])
                tln = P.op("scalar", lambda s: s.activation(
                    rln[:, :], pc[0:2, :], AF.Ln, bias=eps2[:, 0:1],
                    scale=1.0 / 64.0), waits=[tss])
                trs = P.op("scalar", lambda s: s.activation(
                    rstd2r[:, :], rln[:, :], AF.Exp, scale=-0.5), waits=[tln])
                tb2 = P.op("tensor", lambda t: t.matmul(
                    pc[:, :], w_kb2[:, :], rstd2r[:, :], start=True,
                    stop=True), waits=[trs])
                tkcn = P.op("vector", lambda v: v.tensor_mul(
                    qcn_sb[:, :], pc[:, :], kcp[:, :]), waits=[tb2])
                tkr = P.op("tensor", lambda t, ts_=ts_: t.matmul(
                    pc[:, :], w_krbd[:, :], xh2[:, ts_], start=True,
                    stop=True), waits=[tkcn])
                xh2_rd.append(tkr)
                tkrev = P.op("vector", lambda v: v.tensor_copy(
                    qr_sb[:, :], pc[:, :]), waits=[tkr])
                P.op("vector", lambda v, ts_=ts_: v.tensor_copy(
                    K2[0][0:64, ts_], qcn_sb[0:64, :]), waits=[tkcn],
                    token=False)
                P.op("vector", lambda v, ts_=ts_: v.tensor_copy(
                    K2[0][64:128, ts_], qr_sb[0:64, :]), waits=[tkrev],
                    token=False)
                P.op("vector", lambda v, ts_=ts_: v.tensor_copy(
                    K2[1][0:64, ts_], qcn_sb[64:128, :]), waits=[tkcn],
                    token=False)
                tk4 = P.op("vector", lambda v, ts_=ts_: v.tensor_copy(
                    K2[1][64:128, ts_], qr_sb[64:128, :]), waits=[tkrev])
                tvw = P.op("tensor", lambda t: t.matmul(
                    pc[:, :], w_wobd[:, :], vp[:, :], start=True, stop=True),
                    waits=[tv2, tkrev])
                tvwe = P.op("vector", lambda v: v.tensor_copy(
                    vp[:, :], pc[:, :]), waits=[tvw])
                pcF[0] = tvwe

                def vtrs(t):
                    last = None
                    for k in range(4):
                        last = t.transpose(pbt[:, 128 * k:128 * (k + 1)],
                                           vp[:, 128 * k:128 * (k + 1)],
                                           c_ident[:, :])
                    return last
                tvf = P.op("tensor", vtrs, waits=[tvwe, pbt_owner])
                pv = pbt.ap()[:, 0:512].rearrange("p (k h) -> p k h", k=4)
                tve1 = P.op("vector", lambda v, tb=tb: v.tensor_copy(
                    V2[0][:, 4 * tb:4 * (tb + 1), 0:64], pv[:, :, 0:64]),
                    waits=[tvf] + (list(vones_toks)
                                   if tb == 0 and b == 0 else []))
                tve2 = P.op("vector", lambda v, tb=tb: v.tensor_copy(
                    V2[1][:, 4 * tb:4 * (tb + 1), 0:64], pv[:, :, 64:128]),
                    waits=[tve1])
                nonlocal_pbt[0] = tve2
                k_toks[tb] = (tk4, tve2)
            nonlocal_pbt = [pbt_owner]
            pend.append(bundle_k)
            def _sync_pbt(nl=nonlocal_pbt):
                nonlocal pbt_owner
                pbt_owner = nl[0]
            pend.append(_sync_pbt)
        flush_p2()
        return xh2_ld, q_toks, k_toks, xh2_rd

    class FFNFiller:

        def __init__(self):
            self.steps = []
            self.pcf = [None]
            self.prev_tbev = [None]
            self.prev_tg1 = [None]
            self.prev_tg2 = [None]
            self.prev_tsi = [None]
            self.ah_dma = {}
            self.nslot = [0]
            self.tg_tok = [None]
            self.slices = []
            self.kcons = [0]
            self.slice_tok = {}

        def _load_slice(self, k, waits):
            if k >= len(self.slices):
                return
            e, co = self.slices[k]
            sl = k % 4
            self.slice_tok[k] = P.op(
                "sync", lambda sy, e=e, co=co, sl=sl: sy.dma_start(
                    out=w1f[sl][...],
                    in_=w1pT[e].rearrange("(h p) o -> p h o", p=128)[
                        :, :, co:co + 128]),
                waits=waits, n_dma=1, chan="w1f%d" % sl, token=True)

        def build(self, tbs, first_wait):
            self.pcf[0] = pcF[0]
            for tb in tbs:
                ts_ = slice(512 * tb, 512 * (tb + 1))
                for e in range(8):
                    self.steps.append(self._mk_be(tb, e, ts_))
                    for c in range(8):
                        self.slices.append((e, 256 * c))
                        self.slices.append((e, 256 * c + 128))
                        self.steps.append(self._mk_mm(tb, e, c, ts_, False))
                        self.steps.append(self._mk_mm(tb, e, c, ts_, True))
            for k in range(4):
                self._load_slice(k, [first_wait])

        def _mk_be(self, tb, e, ts_):
            def f():
                tbe = P.op("tensor", lambda t, e=e, ts_=ts_: t.matmul(
                    pc[:, :], w_sel[:, e, :], combT[0:8, ts_],
                    start=True, stop=True),
                    waits=[self.pcf[0], self.prev_tg2[0], combT_tok])
                tbev = P.op("scalar", lambda s: s.activation(
                    f_be[:, :], pc[:, :], AF.Copy, scale=1.0 / 1.702),
                    waits=[tbe, self.prev_tg2[0]])
                self.pcf[0] = tbev
                self.prev_tbev[0] = tbev
            return f

        def _mk_mm(self, tb, e, c, ts_, is_lin):
            def f():
                k = self.kcons[0]
                sl = k % 4

                def mmg(t, sl=sl, ts_=ts_):
                    last = None
                    for hc in range(8):
                        last = t.matmul(pc[:, :], w1f[sl][:, hc, :],
                                        nfT[:, hc, ts_], start=(hc == 0),
                                        stop=(hc == 7))
                    return last
                tmm = P.op("tensor", mmg,
                           waits=[self.slice_tok[k], self.pcf[0]])
                self.kcons[0] += 1
                self._load_slice(k + 4, [tmm])
                bcol = 8 * e + c
                if not is_lin:
                    tg = P.op("vector", lambda v, bcol=bcol: v.tensor_scalar(
                        f_g[:, :], pc[:, :], w_b1g[:, bcol:bcol + 1], 7.0,
                        ALU.add, ALU.min), waits=[tmm, self.prev_tsi[0]])
                    self.pcf[0] = tg
                    self.tg_tok[0] = tg
                else:
                    tl = P.op("vector", lambda v, bcol=bcol: v.tensor_scalar(
                        f_l[:, :], pc[:, :], w_b1l[:, bcol:bcol + 1], 7.0,
                        ALU.add, ALU.min), waits=[tmm, self.prev_tg1[0]])
                    self.pcf[0] = tl
                    tl = P.op("vector", lambda v: v.tensor_scalar(
                        f_l[:, :], f_l[:, :], -7.0, 1.0, ALU.max, ALU.add),
                        waits=[tl])
                    tsi = P.op("scalar", lambda s: s.activation(
                        f_si[:, :], f_g[:, :], AF.Silu, scale=1.702),
                        waits=[self.tg_tok[0], self.prev_tg1[0]])
                    self.prev_tsi[0] = tsi
                    tg1 = P.op("gpsimd", lambda g: g.tensor_mul(
                        f_act[:, :], f_si[:, :], f_l[:, :]),
                        waits=[tsi, tl, self.prev_tg2[0]])
                    self.prev_tg1[0] = tg1
                    asl = self.nslot[0] % 2
                    tg2 = P.op("gpsimd", lambda g, asl=asl: g.tensor_mul(
                        f_ah[asl][...], f_act[:, :], f_be[:, :]),
                        waits=[tg1, self.prev_tbev[0],
                               self.ah_dma.get(self.nslot[0] - 2)])
                    self.prev_tg2[0] = tg2
                    tdma = P.op("sync", lambda sy, tb=tb, e=e, c=c, asl=asl:
                                sy.dma_start(
                                    out=ahat_d[tb, e, c],
                                    in_=f_ah[asl][...]),
                                waits=[tg2], n_dma=1, chan="ahf%d" % asl)
                    self.ah_dma[self.nslot[0]] = tdma
                    self.nslot[0] += 1
            return f

        def step(self):
            if self.steps:
                self.steps.pop(0)()

        def drain(self):
            while self.steps:
                self.steps.pop(0)()

    def flash_batch(b, q_toks, k_toks, filler):
        nonlocal pbt_owner, flash_last_pe
        expT = [{}, {}]
        pvT = [{}, {}]
        gH = [0]
        ctx_ev_h = {0: b4F[0], 1: b5F[0]}
        wo_ev = [pcF[0]]
        ctxsb_free = [pcF[0]]
        div_tok = [None]
        last_tf = [None]
        nepi = [0]
        pending = []
        Sslot = [ps4.ap()[:, 0:1024], ps4.ap()[:, 1024:2048]]
        Cslot = [pb4, pb5]

        def emit_epilogue(hh, qb, pv_last):
            ctx_ev = P.op("vector", lambda v, hh=hh: v.tensor_copy(
                ctxsb[:, :], Cslot[hh][0:65, :]),
                waits=[pv_last, ctxsb_free[0]])
            ctx_ev_h[hh] = ctx_ev
            sl = nepi[0] % 2
            nepi[0] += 1

            def otrs(t, sl=sl):
                last = None
                for k in range(4):
                    last = t.transpose(pbt[:, 512 * sl + 68 * k:
                                           512 * sl + 68 * k + 65],
                                       ctxsb[0:65, 128 * k:128 * (k + 1)],
                                       c_ident[0:65, 0:65])
                return last
            tf = P.op("tensor", otrs, waits=[ctx_ev, pbt_owner, div_tok[0]])
            ctxsb_free[0] = tf
            last_tf[0] = tf
            dtok = tf
            slot_guard = att_dma_toks[-2] if len(att_dma_toks) >= 2 else None
            for k in range(4):
                dtok = P.op("vector", lambda v, k=k, sl=sl: v.reciprocal(
                    recip[:, k:k + 1],
                    pbt[:, 512 * sl + 68 * k + 64:512 * sl + 68 * k + 65]),
                    waits=[dtok, slot_guard] if k == 0 else [dtok])
                dtok = P.op("vector", lambda v, k=k, sl=sl: v.tensor_scalar_mul(
                    attst[:, sl, k, :],
                    pbt[:, 512 * sl + 68 * k:512 * sl + 68 * k + 64],
                    recip[:, k:k + 1]), waits=[dtok])
            div_tok[0] = dtok
            rowv = ag2_in.rearrange("(n k p) c -> n k p c", k=4, p=128)
            tdma = P.op("sync", lambda sy, b=b, hh=hh, qb=qb, sl=sl:
                        sy.dma_start(
                            out=rowv[16 * b + qb, :, :,
                                     64 * hh:64 * (hh + 1)]
                            .rearrange("k p c -> p k c"),
                            in_=attst[:, sl, :, :]),
                        waits=[dtok], n_dma=1, chan="attst%d" % sl)
            att_dma_toks.append(tdma)
            return dtok

        def flush_pending():
            nonlocal pbt_owner
            while pending:
                args = pending.pop(0)
                pbt_owner = emit_epilogue(*args)

        ready = [q_toks[-1], k_toks[-1][0], k_toks[-1][1]]
        for qb in range(16):
            for j in range(qb + 1):
                for half in range(2):
                    g = gH[0]
                    Stoks = [None, None]
                    for hh in range(2):
                        wv = list(ready)
                        ready = []
                        if expT[hh].get(g - 1) is not None:
                            wv.append(expT[hh][g - 1])
                        for kt2 in range(2):
                            kt = 2 * half + kt2
                            kg = 4 * j + kt
                            tok = P.op(
                                "tensor",
                                lambda t, hh=hh, qb=qb, kg=kg, kt2=kt2:
                                t.matmul(
                                    Sslot[hh][:, 512 * kt2:512 * (kt2 + 1)],
                                    K2[hh][:, 128 * kg:128 * (kg + 1)],
                                    Q2[hh][:, 512 * qb:512 * (qb + 1)],
                                    start=True, stop=True),
                                waits=wv, token=(kt2 == 1))
                            wv = []
                        Stoks[hh] = tok
                    if j == 0 and half == 0:
                        flush_pending()
                    for hh in range(2):
                        wv = [Stoks[hh], negM_tok]
                        if pvT[hh].get(g - 1) is not None:
                            wv.append(pvT[hh][g - 1])
                        texp = P.op("scalar", lambda s, hh=hh: s.activation(
                            tp[:, hh, :], Sslot[hh][:, :],
                            AF.Exp, bias=negM[:, 0:1], scale=0.125),
                            waits=wv)
                        expT[hh][g] = texp
                        if j == qb:
                            for kt2 in range(2):
                                kt = 2 * half + kt2
                                texp = P.op("vector", lambda v, hh=hh, kt=kt,
                                            kt2=kt2: v.tensor_mul(
                                                tp[:, hh, 512 * kt2:
                                                   512 * (kt2 + 1)],
                                                tp[:, hh, 512 * kt2:
                                                   512 * (kt2 + 1)],
                                                c_masks[:, 384 - 128 * kt:
                                                        896 - 128 * kt]),
                                            waits=[texp])
                            expT[hh][g] = texp
                    filler.step()
                    for hh in range(2):
                        wpv = [expT[hh][g]]
                        if j == 0 and half == 0:
                            wpv.append(ctx_ev_h[hh])
                        for kt2 in range(2):
                            kt = 2 * half + kt2
                            kg = 4 * j + kt
                            tpv = P.op(
                                "tensor",
                                lambda t, hh=hh, kg=kg, kt2=kt2,
                                st_=(j == 0 and half == 0 and kt2 == 0),
                                sp_=(j == qb and kt == 3): t.matmul(
                                    Cslot[hh][0:65, :],
                                    V2[hh][:, kg, 0:65],
                                    tp[:, hh, 512 * kt2:512 * (kt2 + 1)],
                                    start=st_, stop=sp_),
                                waits=wpv, token=(kt2 == 1))
                            wpv = []
                        pvT[hh][g] = tpv
                    gH[0] += 1
            pending.append((0, qb, pvT[0][gH[0] - 1]))
            pending.append((1, qb, pvT[1][gH[0] - 1]))
        flush_pending()
        filler.drain()
        b4F[0] = ctx_ev_h[0]
        b5F[0] = ctx_ev_h[1]
        flash_last_pe = last_tf[0]
        return flash_last_pe

    fd = None
    ah_done = []
    for b in range(B):
        xh2_ld, q_toks, k_toks, xh2_rd = p2_batch(b, fd)
        xh2_readers_prev.clear()
        xh2_readers_prev.extend([xh2_rd[-1]])
        filler = FFNFiller()
        filler.build([2 * b, 2 * b + 1], xh2_rd[-1])
        fd = flash_batch(b, q_toks, k_toks, filler)
        xh2_readers_prev.append(filler.ah_dma[filler.nslot[0] - 1])
        xh2_readers_prev.append(filler.ah_dma[filler.nslot[0] - 2])
        pcF[0] = filler.pcf[0]
        ah_done.append(filler.ah_dma[filler.nslot[0] - 1])
        ah_done.append(filler.ah_dma[filler.nslot[0] - 2])

    cc2_tok = P.collective(lambda g: g.collective_compute(
        "AllGather", ALU.bypass, replica_groups=[list(range(NCORE))],
        ins=[ag2_in[:, :]], outs=[ag2_out[:, :]]), waits=att_dma_toks,
        chan="cc2")

    pid_g = nc.partition_id(engines=[ET.Pool])

    def p4():
        nonlocal pbt_owner
        w2_free = [fd] * 6
        pcpw2 = [pcF[0], b5F[0]]
        comb_toks = [None] * 8
        fin_prev = []
        out_toks = []
        prev_p2_last = [fd]
        last_readers = [fd] * 8
        prev_ah = [None] * 8
        for tb in range(4):
            ts_ = slice(512 * tb, 512 * (tb + 1))
            ah_toks = []
            for e in range(8):
                wv_ah = [last_readers[e]] + (ah_done if tb == 0 else [])
                if e >= 4:
                    wv_ah.append(ah_toks[e - 4])
                elif prev_ah[e + 4] is not None:
                    wv_ah.append(prev_ah[e + 4])
                ah_toks.append(P.op(
                    "sync", lambda sy, tb=tb, e=e: sy.dma_start(
                        out=ahat[:, e, :, :],
                        in_=ahat_d[tb, e].rearrange("c p t -> p c t")),
                    waits=wv_ah, n_dma=1, chan="ahin%d" % (e % 4)))
            for hc in range(8):
                pg = hc % 2
                dst = pc if pg == 0 else None
                wv = [combT_tok, pcpw2[pg]]
                dsta = pc[:, :] if pg == 0 else pb5[:, :]
                tbias = P.op("tensor", lambda t, hc=hc, dsta=dsta, ts_=ts_:
                             t.matmul(dsta, w_b2[0:8, 128 * hc:
                                               128 * (hc + 1)],
                                      combT[0:8, ts_], start=True, stop=False),
                             waits=wv, token=False)
                last = None
                for e in range(8):
                    sl = (8 * hc + e) % 6
                    tld = P.op("sync", lambda sy, e=e, hc=hc, sl=sl:
                               sy.dma_start(
                                   out=w2slots[sl][...],
                                   in_=w2T[e].rearrange(
                                       "(i p) h -> p i h", p=128)[
                                       :, :, 128 * hc:128 * (hc + 1)]),
                               waits=[w2_free[sl]], n_dma=1,
                               chan="w2r%d" % sl)

                    def p2mm(t, e=e, sl=sl, dsta=dsta, last_e=(e == 7)):
                        last = None
                        for ic in range(8):
                            last = t.matmul(dsta, w2slots[sl][:, ic, :],
                                            ahat[:, e, ic, :], start=False,
                                            stop=(last_e and ic == 7))
                        return last
                    last = P.op("tensor", p2mm, waits=[tld, ah_toks[e]])
                    w2_free[sl] = last
                    if hc == 7:
                        last_readers[e] = last
                if pg == 0:
                    tcev = P.op("scalar", lambda s, hc=hc, dsta=dsta:
                                s.activation(combsb[:, hc, :], dsta, AF.Copy),
                                waits=[last])
                else:
                    tcev = P.op("vector", lambda v, hc=hc, dsta=dsta:
                                v.tensor_copy(combsb[:, hc, :], dsta),
                                waits=[last])
                pcpw2[pg] = tcev
                comb_toks[hc] = tcev
                prev_p2_last[0] = last
            prev_ah = ah_toks
            for u in range(4):
                t_ = 4 * tb + u
                b, ti = t_ // 8, t_ % 8

                def ctrs(t, u=u):
                    last = None
                    for hc in range(8):
                        last = t.transpose(pbt[:, 128 * hc:128 * (hc + 1)],
                                           combsb[:, hc, 128 * u:
                                                  128 * (u + 1)],
                                           c_ident[:, :])
                    return last
                tf = P.op("tensor", ctrs, waits=comb_toks + [pbt_owner])
                wsf = [tf] + ([fin_prev[-1]] if fin_prev else [])
                ts1 = P.op("vector", lambda v, t_=t_: v.scalar_tensor_tensor(
                    sfin[:, 0:512], pbt[:, 0:512], 1.0,
                    xc[:, t_, 0:512], ALU.mult, ALU.add), waits=wsf)
                ts2 = P.op("vector", lambda v, t_=t_: v.scalar_tensor_tensor(
                    sfin[:, 512:1024], pbt[:, 512:1024], 1.0,
                    xc[:, t_, 512:1024], ALU.mult, ALU.add), waits=[ts1])
                pbt_owner = ts2
                sl = t_ % 2
                wx = [fin_prev[t_ - 2]] if t_ >= 2 else [xc_toks[14 + t_], fd]
                txl = P.op("sync", lambda sy, b=b, ti=ti, t_=t_:
                           sy.dma_start(out=xr[:, t_ % 2, :],
                                        in_=x_own[b].rearrange(
                                            "(n p) d -> n p d", p=128)[ti]),
                           waits=wx, n_dma=1, chan="x%d" % (t_ % 2))
                ag2v = ag2_out.rearrange("(r b2 q u p) c -> q b2 u p r c",
                                         r=NCORE, b2=B, q=8, u=8)
                wa = [cc2_tok] + ([fin_prev[t_ - 1]] if t_ >= 1 else [])
                tal = P.op("gpsimd", lambda g, b=b, ti=ti:
                           g.dma_start(
                               out=xatt.ap().rearrange(
                                   "p (r c) -> p r c", r=8),
                               in_=ag2v[pid_g, b, ti]),
                           waits=wa, n_dma=1, chan="ag2g")
                wf = [txl, tal] + ([out_toks[t_ - 1]] if t_ >= 1 else [])
                tff = P.op("vector", lambda v, t_=t_: v.tensor_add(
                    ffin[:, :], xr[:, t_ % 2, :], xatt[:, :]),
                    waits=wf)
                tff = P.op("vector", lambda v: v.tensor_add(
                    ffin[:, :], ffin[:, :], sfin[:, :]),
                    waits=[tff, ts2])
                fin_prev.append(tff)
                tout = P.op("sync", lambda sy, b=b, ti=ti:
                            sy.dma_start(
                                out=out_own[b].rearrange(
                                    "(n p) d -> n p d", p=128)[ti],
                                in_=ffin[:, :]), waits=[tff], n_dma=1,
                            chan="outd")
                out_toks.append(tout)
        return out_toks

    if stop_after != "flash":
        out_toks = p4()
        P.op("sync", lambda sy: sy.nop(), waits=[out_toks[-1]], token=False)

    sem_handles = {}
    with st:
        for e in ENGS:
            sem_handles[e] = st.enter_context(nc.semaphore("sem_" + e))
        for ch in sorted(P.chans):
            sem_handles[ch] = st.enter_context(nc.semaphore("sem_" + ch))
        with nc.Block() as block:
            P.emit(block, sem_handles)
    return nc


def kernel(**inputs):
    nc = build_program()
    in_maps = [_host_inputs(inputs, c) for c in range(NCORE)]
    res = run_bass_kernel_spmd(nc, in_maps, core_ids=list(range(NCORE)))
    out = np.empty((B, TSEQ, 1024), np.float32)
    for c in range(NCORE):
        o = res.results[c]["out_own"]
        for b in range(B):
            out[b, TOKC * c:TOKC * (c + 1)] = o[b]
    return out
